# version 01
# baseline (speedup 1.0000x reference)
import math
import numpy as np
import concourse.bass as bass
import concourse.mybir as mybir
from concourse.bass_utils import run_bass_kernel_spmd

F32 = mybir.dt.float32
BF16 = mybir.dt.bfloat16
I32 = mybir.dt.int32
AF = mybir.ActivationFunctionType
ALU = mybir.AluOpType
AX = mybir.AxisListType

NCORE = 8
B = 2
L = 8192
D = 1024
TPC = 2048
NDT = D // 128

_ENG = ("sync", "tensor", "vector", "scalar", "gpsimd")


class Prog:

    def __init__(self, name="prog"):
        self.nc = bass.Bass("TRN2", target_bir_lowering=False)
        self.name = name
        self._consts = {}
        self._sb = {}
        self._pending = []
        self._dma_ct = {e: 0 for e in _ENG}
        self._bar_n = 0
        self._ctxs = []

    def input(self, name, shape, dtype=F32):
        return self.nc.dram_tensor(name, list(shape), dtype, kind="ExternalInput").ap()

    def output(self, name, shape, dtype=F32):
        return self.nc.dram_tensor(name, list(shape), dtype, kind="ExternalOutput").ap()

    def scratch(self, name, shape, dtype=F32):
        return self.nc.dram_tensor(name, list(shape), dtype).ap()

    def const(self, name, arr):
        arr = np.asarray(arr)
        if arr.dtype == np.dtype(np.float32):
            dt = F32
        elif arr.dtype == np.dtype(np.int32):
            dt = I32
        else:
            import ml_dtypes
            if arr.dtype == ml_dtypes.bfloat16:
                dt = BF16
            else:
                raise ValueError(arr.dtype)
        self._consts[name] = arr
        return self.nc.dram_tensor(name, list(arr.shape), dt, kind="ExternalInput").ap()

    def sb(self, name, shape, dtype=F32):
        h = self.nc.sbuf_tensor(name, list(shape), dtype)
        t = h.__enter__()
        self._ctxs.append(h)
        return t

    def ps(self, name, shape, dtype=F32):
        h = self.nc.psum_tensor(name, list(shape), dtype)
        t = h.__enter__()
        self._ctxs.append(h)
        return t

    def start(self):
        h = self.nc.semaphore("bar")
        self.bar_sem = h.__enter__()
        self._ctxs.append(h)
        self._dma_sems = {}
        for e in _ENG:
            hh = self.nc.semaphore(f"dma_{e}")
            self._dma_sems[e] = hh.__enter__()
            self._ctxs.append(hh)
        self._dma_tot = {e: 0 for e in _ENG}
        self._blk_h = self.nc.Block()
        self.blk = self._blk_h.__enter__()

    def op(self, eng, fn):
        self._pending.append((eng, fn))

    def dma(self, eng, out, in_):
        def fn(e, out=out, in_=in_):
            return e.dma_start(out=out, in_=in_).then_inc(self._dma_sems[eng], 16)
        self._pending.append((eng, fn))
        self._dma_ct[eng] += 1

    def barrier(self):
        per = {e: [] for e in _ENG}
        for eng, fn in self._pending:
            per[eng].append(fn)
        self._pending = []
        self._bar_n += 1
        target = self._bar_n * len(_ENG)
        for e in _ENG:
            fns = per[e]
            dmas = self._dma_ct[e]
            self._dma_ct[e] = 0
            self._dma_tot[e] += dmas
            tot16 = self._dma_tot[e] * 16
            bar = self.bar_sem
            dsem = self._dma_sems[e]

            def body(eng_obj, fns=fns, dmas=dmas, tot16=tot16, target=target,
                     bar=bar, dsem=dsem):
                last = None
                for fn in fns:
                    r = fn(eng_obj)
                    if r is not None:
                        last = r
                if dmas:
                    eng_obj.wait_ge(dsem, tot16)
                if last is not None and not dmas:
                    last.then_inc(bar, 1)
                else:
                    eng_obj.sem_inc(bar, 1)
                eng_obj.wait_ge(bar, target)

            getattr(self.blk, e)(body)

    def finish(self):
        if self._pending:
            self.barrier()
        self._blk_h.__exit__(None, None, None)
        for h in reversed(self._ctxs):
            h.__exit__(None, None, None)

    def run(self, in_maps):
        maps = []
        for m in in_maps:
            m = dict(m)
            for k, v in self._consts.items():
                m[k] = v
            maps.append(m)
        res = run_bass_kernel_spmd(self.nc, maps, core_ids=list(range(NCORE)))
        return res.results



class Aux:

    def __init__(self, p: Prog, T=TPC, psum_mode="split"):
        self.p = p
        self.T = T
        self.ones_col = p.sb("ones_col", [128, 1], F32)
        self.ones_col_bf = p.sb("ones_col_bf", [128, 1], BF16)
        self.ones_row = p.sb("ones_row", [1, 128], F32)
        self.ones_row_bf = p.sb("ones_row_bf", [1, 128], BF16)
        self.ident_bf = p.sb("ident_bf", [128, 128], BF16)
        self.ident_f32 = p.sb("ident_f32", [128, 128], F32)
        self._ident_in = p.const("c_ident_bf", np.eye(128, dtype=np.float32))
        if psum_mode == "split":
            self.ps_f32 = p.ps("ps_pool_f32", [128, 2048], F32)
            self.ps_bf = p.ps("ps_pool_bf", [128, 4096], BF16)
            self.ps_tr = [self.ps_bf[:, 2048:3072], self.ps_bf[:, 3072:4096]]
        else:
            self.ps_f32 = p.ps("ps_pool_f32", [128, 4096], F32)

    def init_phase(self):
        p = self.p
        p.op("vector", lambda v: v.memset(self.ones_col[:, :], 1.0))
        p.op("vector", lambda v: v.memset(self.ones_col_bf[:, :], 1.0))
        p.op("vector", lambda v: v.memset(self.ones_row[:, :], 1.0))
        p.op("vector", lambda v: v.memset(self.ones_row_bf[:, :], 1.0))
        p.dma("gpsimd", self.ident_bf[:, :], self._ident_in[:, :])
        p.dma("sync", self.ident_f32[:, :], self._ident_in[:, :])
        p.barrier()


def transpose_pipe(p, aux, groups, extra_per_group=None):
    ng = len(groups)
    for g in range(ng + 1):
        if g < ng:
            pcs = groups[g]
            assert len(pcs) <= 8
            bank = aux.ps_tr[g % 2]
            for j, (src, dst) in enumerate(pcs):
                p.op("tensor", lambda t, src=src, j=j, bank=bank: t.transpose(
                    bank[:, j * 128:(j + 1) * 128], src, aux.ident_bf[:, :]))
        if g > 0:
            pcs = groups[g - 1]
            bank = aux.ps_tr[(g - 1) % 2]
            for j, (src, dst) in enumerate(pcs):
                p.op("vector", lambda v, dst=dst, j=j, bank=bank: v.tensor_copy(
                    dst, bank[:, j * 128:(j + 1) * 128]))
        if extra_per_group is not None and g < ng:
            extra_per_group(g)
        p.barrier()


def fm_to_tm_groups(Xfm_bf, Xtm, T, n_d=NDT):
    groups = []
    for ti in range(T // 128):
        pcs = []
        for di in range(n_d):
            src = Xfm_bf[:, di * T + ti * 128: di * T + (ti + 1) * 128]
            dst = Xtm[:, ti * (n_d * 128) + di * 128: ti * (n_d * 128) + (di + 1) * 128]
            pcs.append((src, dst))
        groups.append(pcs)
    return groups


def tm_to_fm_groups(Xtm, Xfm, T, n_d=NDT):
    groups = []
    for di in range(n_d):
        pcs = []
        for ti in range(T // 128):
            src = Xtm[:, ti * (n_d * 128) + di * 128: ti * (n_d * 128) + (di + 1) * 128]
            dst = Xfm[:, di * T + ti * 128: di * T + (ti + 1) * 128]
            pcs.append((src, dst))
        if len(pcs) > 8:
            half = len(pcs) // 2
            groups.append(pcs[:half])
            groups.append(pcs[half:])
        else:
            groups.append(pcs)
    return groups


def ln_tm(p, aux, Xtm, XNtm, stats, sq, T, n_d=NDT, eps=1e-5, d_tot=None):
    d_tot = d_tot or n_d * 128
    ntt = T // 128
    W = n_d * 128
    for ti in range(ntt + 1):
        if ti < ntt:
            sl = slice(ti * W, (ti + 1) * W)
            p.op("scalar", lambda s, sl=sl, ti=ti: s.activation(
                out=sq[ti % 2][:, 0:W], in_=Xtm[:, sl], func=AF.Square))
            p.op("vector", lambda v, sl=sl, ti=ti: v.reduce_sum(
                stats[:, ti:ti + 1], Xtm[:, sl], axis=AX.X))
        if ti > 0:
            p.op("vector", lambda v, ti=ti: v.reduce_sum(
                stats[:, ntt + ti - 1:ntt + ti], sq[(ti - 1) % 2][:, 0:W], axis=AX.X))
        p.barrier()
    inv = 1.0 / d_tot
    p.op("vector", lambda v: v.tensor_scalar_mul(
        stats[:, 2 * ntt:3 * ntt], stats[:, 0:ntt], -inv))
    p.op("scalar", lambda s: s.activation(
        out=stats[:, 0:ntt], in_=stats[:, 0:ntt], func=AF.Square))
    p.barrier()
    p.op("vector", lambda v: v.tensor_scalar(
        stats[:, ntt:2 * ntt], stats[:, ntt:2 * ntt], inv, None, ALU.mult))
    p.op("vector", lambda v: v.tensor_scalar(
        stats[:, 0:ntt], stats[:, 0:ntt], inv * inv, None, ALU.mult))
    p.barrier()
    p.op("vector", lambda v: v.tensor_sub(
        stats[:, ntt:2 * ntt], stats[:, ntt:2 * ntt], stats[:, 0:ntt]))
    p.barrier()
    p.op("vector", lambda v: v.tensor_scalar_add(
        stats[:, ntt:2 * ntt], stats[:, ntt:2 * ntt], float(eps)))
    p.barrier()
    p.op("scalar", lambda s: s.activation(
        out=stats[:, ntt:2 * ntt], in_=stats[:, ntt:2 * ntt], func=AF.Ln))
    p.barrier()
    p.op("scalar", lambda s: s.activation(
        out=stats[:, 3 * ntt:4 * ntt], in_=stats[:, ntt:2 * ntt], func=AF.Exp, scale=-0.5))
    p.barrier()
    for ti in range(ntt):
        sl = slice(ti * W, (ti + 1) * W)
        p.op("vector", lambda v, sl=sl, ti=ti: v.tensor_scalar(
            XNtm[:, sl], Xtm[:, sl],
            stats[:, 2 * ntt + ti:2 * ntt + ti + 1],
            stats[:, 3 * ntt + ti:3 * ntt + ti + 1],
            ALU.add, ALU.mult))
    p.barrier()





def build_ln():
    p = Prog("ln")
    x_in = p.input("x_fm", (128, NDT * TPC))
    o_xn_tm = p.output("xn_tm", (TPC, D), BF16)
    o_xn_fm = p.output("xn_fm", (128, NDT * TPC), BF16)
    o_m = p.output("m_part", (64, TPC), F32)
    o_lnr = p.output("lnrows", (2, TPC // 128, 128), F32)

    aux = Aux(p)
    Xbf = p.sb("Xbf", [128, NDT * TPC], BF16)
    Xtm = p.sb("Xtm", [128, (TPC // 128) * D], BF16)
    XNtm = p.sb("XNtm", [128, (TPC // 128) * D], BF16)
    XNfm = p.sb("XNfm", [128, NDT * TPC], BF16)
    stats = p.sb("stats", [128, 4 * (TPC // 128)], F32)
    sq = [p.sb(f"sq{i}", [128, D], F32) for i in range(2)]
    mP = p.const("c_mP", np.concatenate([np.eye(64), np.eye(64)], axis=0).astype(np.float32))
    mP_sb = p.sb("mP_sb", [128, 64], BF16)
    m_sb = p.sb("m_sb", [64, TPC], F32)

    p.start()
    aux.init_phase()
    p.dma("gpsimd", Xbf[:, :], x_in[:, :])
    p.dma("gpsimd", mP_sb[:, :], mP[:, :])
    p.barrier()
    transpose_pipe(p, aux, fm_to_tm_groups(Xbf, Xtm, TPC))
    ln_tm(p, aux, Xtm, XNtm, stats, sq, TPC)
    transpose_pipe(p, aux, tm_to_fm_groups(XNtm, XNfm, TPC))
    for ci in range(TPC // 512):
        for di in range(NDT):
            p.op("tensor", lambda t, ci=ci, di=di: t.matmul(
                aux.ps_f32[0:64, 0:512], mP_sb[:, :],
                XNfm[:, di * TPC + ci * 512: di * TPC + (ci + 1) * 512],
                start=(di == 0), stop=(di == NDT - 1)))
        p.barrier()
        p.op("vector", lambda v, ci=ci: v.tensor_copy(
            m_sb[:, ci * 512:(ci + 1) * 512], aux.ps_f32[0:64, 0:512]))
        p.barrier()
    p.dma("sync", o_xn_tm.rearrange("(a p) c -> p a c", p=128),
          XNtm[:, :].rearrange("p (a c) -> p a c", c=D))
    p.dma("gpsimd", o_xn_fm[:, :], XNfm[:, :])
    p.dma("scalar", o_m[:, :], m_sb[:, :])
    ntt = TPC // 128
    for ti in range(ntt):
        p.dma("scalar", o_lnr[0, ti, :], stats[:, 2 * ntt + ti: 2 * ntt + ti + 1])
        p.dma("scalar", o_lnr[1, ti, :], stats[:, 3 * ntt + ti: 3 * ntt + ti + 1])
    p.barrier()
    p.finish()
    return p



def build_l0():
    p = Prog("l0")
    x_in = p.input("x_fm", (128, NDT * TPC))
    xn_in = p.input("xn_fm", (128, NDT * TPC), BF16)
    xng_in = p.input("xn_g", (L, D), BF16)
    mo_in = p.input("m_own_parts", (2, 64, TPC), F32)
    ma_in = p.input("m_all_parts", (8, 64, TPC), F32)
    w_win = p.input("w_win", (D, 64))
    w_bin = p.input("w_bin", (64, 1))
    w_W1 = p.input("w_W1", (64, 16))
    w_b1 = p.input("w_b1", (64, 16))
    w_W2 = p.input("w_W2", (64, 16))
    w_b2 = p.input("w_b2", (64, 1))
    w_wout = p.input("w_wout", (64, D))
    w_bout = p.input("w_bout", (128, NDT))
    o_x1 = p.output("x1_fm", (128, NDT * TPC))

    aux = Aux(p)
    NTT = TPC // 128
    XN = p.sb("XN", [128, NDT * TPC], BF16)
    GB = [p.sb("GB0", [128, 16 * D], BF16)]
    OUT = p.sb("OUT", [128, NTT * D], BF16)
    m_bf = p.sb("m_bf", [64, L], BF16)
    m_own = p.sb("m_own", [64, TPC], BF16)
    xchunk = [p.sb(f"xchunk{i}", [128, TPC], F32) for i in range(3)]
    Sg = p.sb("Sg", [128, 2048], BF16)
    Sg2 = p.sb("Sg2", [128, 2048], BF16)
    PT = p.sb("PT", [128, 2048], BF16)
    norm_row = p.sb("norm_row", [1, TPC], F32)
    recip_c = p.sb("recip_c", [128, 16], F32)
    s_sb = p.sb("s_sb", [64, TPC], F32)
    gg = [p.sb(f"gg{i}", [64, TPC], F32) for i in range(2)]
    g_bf = p.sb("g_bf", [64, TPC], BF16)
    win_bf = p.sb("win_bf", [128, NDT * 64], BF16)
    wout_bf = p.sb("wout_bf", [64, D], BF16)
    bin_c = p.sb("bin_c", [64, 1], F32)
    W1c = p.sb("W1c", [64, 16], F32)
    b1c = p.sb("b1c", [64, 16], F32)
    W2c = p.sb("W2c", [64, 16], F32)
    b2c = p.sb("b2c", [64, 1], F32)
    bout_c = p.sb("bout_c", [128, NDT], F32)
    h_tmp = p.sb("h_tmp", [64, TPC], F32)

    p.start()
    aux.init_phase()
    p.dma("sync", XN[:, :], xn_in[:, :])
    p.dma("gpsimd", win_bf[:, :].rearrange("p (a c) -> p a c", c=64),
          w_win.rearrange("(a p) c -> p a c", p=128))
    p.dma("gpsimd", wout_bf[:, :], w_wout[:, :])
    p.dma("scalar", bin_c[:, :], w_bin[:, :])
    p.dma("scalar", W1c[:, :], w_W1[:, :])
    p.dma("scalar", b1c[:, :], w_b1[:, :])
    p.dma("scalar", W2c[:, :], w_W2[:, :])
    p.dma("scalar", b2c[:, :], w_b2[:, :])
    p.dma("scalar", bout_c[:, :], w_bout[:, :])
    p.dma("sync", xchunk[0][0:64, :], mo_in[0, :, :])
    p.dma("sync", xchunk[1][0:64, :], mo_in[1, :, :])
    p.dma("sync", GB[0][:, :].rearrange("p (a c) -> p a c", c=D),
          xng_in.rearrange("(s a p) c -> s p a c", s=4, p=128)[0])
    p.barrier()
    p.op("vector", lambda v: v.tensor_add(xchunk[0][0:64, :], xchunk[0][0:64, :], xchunk[1][0:64, :]))
    p.barrier()
    p.op("vector", lambda v: v.tensor_scalar(m_own[:, :], xchunk[0][0:64, :], 1.0 / 32.0, None, ALU.mult))
    p.barrier()
    for s in range(4):
        p.dma("sync", xchunk[0][0:64, :], ma_in[s, :, :])
        p.dma("sync", xchunk[1][0:64, :], ma_in[4 + s, :, :])
        p.barrier()
        p.op("vector", lambda v: v.tensor_add(xchunk[0][0:64, :], xchunk[0][0:64, :], xchunk[1][0:64, :]))
        p.barrier()
        p.op("vector", lambda v, s=s: v.tensor_scalar(
            m_bf[:, s * TPC:(s + 1) * TPC], xchunk[0][0:64, :], 1.0 / 32.0, None, ALU.mult))
        p.barrier()

    for ci in range(4):
        for di in range(NDT):
            p.op("tensor", lambda t, ci=ci, di=di: t.matmul(
                aux.ps_f32[0:64, ci * 512:(ci + 1) * 512],
                win_bf[:, di * 64:(di + 1) * 64],
                XN[:, di * TPC + ci * 512: di * TPC + (ci + 1) * 512],
                start=(di == 0), stop=(di == NDT - 1)))
    p.barrier()
    p.op("vector", lambda v: v.tensor_scalar(
        s_sb[:, :], aux.ps_f32[0:64, 0:2048], bin_c[:, 0:1], None, ALU.add))
    p.op("vector", lambda v: v.memset(gg[1][:, :], 0.0))
    p.barrier()
    for i in range(16):
        p.op("scalar", lambda s, i=i: s.activation(
            out=h_tmp[:, :], in_=s_sb[:, :], func=AF.Relu,
            bias=b1c[:, i:i + 1], scale=W1c[:, i:i + 1]))
        p.barrier()
        p.op("vector", lambda v, i=i: v.scalar_tensor_tensor(
            gg[i % 2][:, :], h_tmp[:, :], W2c[:, i:i + 1], gg[(i + 1) % 2][:, :],
            ALU.mult, ALU.add))
        p.barrier()
    p.op("vector", lambda v: v.tensor_scalar(
        gg[15 % 2][:, :], gg[15 % 2][:, :], b2c[:, 0:1], None, ALU.add))
    p.barrier()
    p.op("vector", lambda v: v.tensor_copy(g_bf[:, :], gg[15 % 2][:, :]))
    p.barrier()

    PI = float(np.pi)
    for sb in range(4):
        if sb > 0:
            p.dma("sync", GB[0][:, :].rearrange("p (a c) -> p a c", c=D),
                  xng_in.rearrange("(s a p) c -> s p a c", s=4, p=128)[sb])
            p.barrier()
        for ib in range(NTT):
            i0 = ib * 128
            for jb in range(16):
                p.op("tensor", lambda t, sb=sb, jb=jb, i0=i0: t.matmul(
                    aux.ps_f32[:, jb * 128:(jb + 1) * 128],
                    m_bf[:, sb * 2048 + jb * 128: sb * 2048 + (jb + 1) * 128],
                    m_own[:, i0:i0 + 128],
                    start=True, stop=True))
            p.barrier()
            p.op("vector", lambda v: v.tensor_scalar(
                Sg[:, :], aux.ps_f32[:, 0:2048], -PI, PI, ALU.max, ALU.min))
            p.barrier()
            p.op("scalar", lambda s: s.activation(out=Sg[:, :], in_=Sg[:, :], func=AF.Sin, scale=0.5))
            p.barrier()
            p.op("vector", lambda v: v.tensor_mul(Sg2[:, :], Sg[:, :], Sg[:, :]))
            p.barrier()
            p.op("scalar", lambda s: s.activation(out=PT[:, :], in_=Sg2[:, :], func=AF.Exp))
            p.barrier()
            for jb in range(16):
                for h2 in range(2):
                    p.op("tensor", lambda t, jb=jb, h2=h2: t.matmul(
                        aux.ps_f32[:, h2 * 512:(h2 + 1) * 512],
                        PT[:, jb * 128:(jb + 1) * 128],
                        GB[0][:, jb * D + h2 * 512: jb * D + (h2 + 1) * 512],
                        start=(jb == 0), stop=(jb == 15)))
                p.op("tensor", lambda t, jb=jb: t.matmul(
                    aux.ps_f32[0:1, 1024:1152],
                    aux.ones_col_bf[:, 0:1],
                    PT[:, jb * 128:(jb + 1) * 128],
                    start=(jb == 0), stop=(jb == 15)))
            p.barrier()
            if sb == 0:
                p.op("vector", lambda v, ib=ib: v.tensor_copy(
                    OUT[:, ib * D:(ib + 1) * D], aux.ps_f32[:, 0:1024]))
                p.op("vector", lambda v, ib=ib: v.tensor_copy(
                    norm_row[0:1, ib * 128:(ib + 1) * 128], aux.ps_f32[0:1, 1024:1152]))
            else:
                p.op("vector", lambda v, ib=ib: v.tensor_add(
                    OUT[:, ib * D:(ib + 1) * D], OUT[:, ib * D:(ib + 1) * D],
                    aux.ps_f32[:, 0:1024]))
                p.op("vector", lambda v, ib=ib: v.tensor_add(
                    norm_row[0:1, ib * 128:(ib + 1) * 128],
                    norm_row[0:1, ib * 128:(ib + 1) * 128], aux.ps_f32[0:1, 1024:1152]))
            p.barrier()

    for ib in range(16):
        p.op("tensor", lambda t, ib=ib: t.transpose(
            aux.ps_f32[:, 1536:1537],
            norm_row[0:1, ib * 128:(ib + 1) * 128], aux.ident_f32[0:1, 0:1]))
        p.barrier()
        p.op("vector", lambda v, ib=ib: v.tensor_copy(recip_c[:, ib:ib + 1], aux.ps_f32[:, 1536:1537]))
        p.barrier()
    p.op("vector", lambda v: v.reciprocal(recip_c[:, 0:16], recip_c[:, 0:16]))
    p.barrier()
    for ib in range(16):
        p.op("vector", lambda v, ib=ib: v.tensor_scalar(
            OUT[:, ib * D:(ib + 1) * D], OUT[:, ib * D:(ib + 1) * D],
            recip_c[:, ib:ib + 1], None, ALU.mult))
    p.barrier()

    p.dma("sync", xchunk[0][:, :], x_in[:, 0:TPC])
    p.barrier()
    for di in range(NDT):
        for ti in range(16):
            p.op("tensor", lambda t, ti=ti, di=di: t.transpose(
                aux.ps_bf[:, 2048 + ti * 128: 2048 + (ti + 1) * 128],
                OUT[:, ti * D + di * 128: ti * D + di * 128 + 128],
                aux.ident_bf[:, :]))
        for ch in range(4):
            p.op("tensor", lambda t, di=di, ch=ch: t.matmul(
                aux.ps_f32[:, ch * 512:(ch + 1) * 512],
                wout_bf[:, di * 128:(di + 1) * 128],
                g_bf[:, ch * 512:(ch + 1) * 512],
                start=True, stop=True))
        if di + 1 < NDT:
            p.dma("sync", xchunk[(di + 1) % 3][:, :], x_in[:, (di + 1) * TPC:(di + 2) * TPC])
        p.barrier()
        p.op("vector", lambda v, di=di: v.tensor_add(
            xchunk[di % 3][:, :], xchunk[di % 3][:, :], aux.ps_bf[:, 2048:4096]))
        p.op("vector", lambda v, di=di: v.tensor_add(
            xchunk[di % 3][:, :], xchunk[di % 3][:, :], aux.ps_f32[:, 0:2048]))
        p.op("vector", lambda v, di=di: v.tensor_scalar(
            xchunk[di % 3][:, :], xchunk[di % 3][:, :], bout_c[:, di:di + 1], None, ALU.add))
        p.barrier()
        p.dma("gpsimd", o_x1[:, di * TPC:(di + 1) * TPC], xchunk[di % 3][:, :])
    p.barrier()
    p.finish()
    return p




_PROGS = {}


def _get_prog(name, builder):
    if name not in _PROGS:
        _PROGS[name] = builder()
    return _PROGS[name]


def _slice_of_core(c):
    return c // 4, slice((c % 4) * TPC, (c % 4 + 1) * TPC)


def host_fm(x_btd):
    out = []
    for c in range(NCORE):
        b, sl = _slice_of_core(c)
        xs = np.ascontiguousarray(x_btd[b, sl].T)
        fm = xs.reshape(NDT, 128, TPC).transpose(1, 0, 2).reshape(128, NDT * TPC)
        out.append(np.ascontiguousarray(fm))
    return out


def unfm(a):
    T = a.shape[1] // NDT
    return a.reshape(128, NDT, T).transpose(1, 0, 2).reshape(NDT * 128, T)


DI = 2048
NH = 32
TCH = 128
NCH = TPC // TCH
HWD = 2176


def build_l1a1():
    p = Prog("l1a1")
    xnh = p.input("xn1h", (128, NDT * HWD), BF16)
    wproj = p.input("w_proj", (D, 4416))
    wc7 = p.input("w_conv", (128, 18 * 7))
    wcb = p.input("w_convb", (128, 18))
    dtb = p.input("w_dtb", (64, 1))
    alog = p.input("w_alog", (64, 1))
    fcd = p.input("w_fcd", (DI, 32))
    dpar = p.input("w_dpar", (32, 1))
    crep = p.const("c_rep", _mk_crep())
    o_x2 = p.output("x2_fm", (128, 16 * TPC), BF16)
    o_bc = p.output("bc_fm", (128, 2 * TPC), BF16)
    o_dt = p.output("dt", (64, TPC))
    o_z = p.output("z_act", (128, 16 * TPC), BF16)
    o_skip = p.output("skip", (128, 16 * TPC), BF16)
    o_cedge = p.output("c_edge", (2, 64))

    aux = Aux(p)
    XBC = p.sb("XBC", [128, 18 * TPC], BF16)
    prj = [p.sb(f"prj{i}", [128, HWD], BF16) for i in range(2)]
    wst = [p.sb(f"wst{i}", [128, NDT * 128], BF16) for i in range(2)]
    cacc = [p.sb(f"cacc{i}", [128, TPC], F32) for i in range(2)]
    zst = p.sb("zst", [128, TPC], BF16)
    dtr = p.sb("dtr", [64, TPC], F32)
    dts = [p.sb(f"dts{i}", [64, TPC], F32) for i in range(2)]
    wc7s = p.sb("wc7s", [128, 18 * 7], F32)
    wcbs = p.sb("wcbs", [128, 18], F32)
    dtbs = p.sb("dtbs", [64, 1], F32)
    alogs = p.sb("alogs", [64, 1], F32)
    fcds = p.sb("fcds", [128, 16 * 32], BF16)
    dpars = p.sb("dpars", [32, 1], F32)
    creps = p.sb("creps", [32, TPC], BF16)
    fc_sb = p.sb("fc_sb", [32, TPC], F32)
    fc_bf = p.sb("fc_bf", [32, TPC], BF16)
    skst = p.sb("skst", [128, TPC], BF16)

    XNH = p.sb("XNH", [128, NDT * HWD], BF16)
    p.start()
    aux.init_phase()
    p.dma("sync", XNH[:, :], xnh[:, :])
    p.dma("gpsimd", wc7s[:, :], wc7[:, :])
    p.dma("gpsimd", wcbs[:, :], wcb[:, :])
    p.dma("gpsimd", dtbs[:, :], dtb[:, :])
    p.dma("gpsimd", alogs[:, :], alog[:, :])
    p.dma("gpsimd", fcds[:, :].rearrange("p (a c) -> p a c", c=32),
          fcd.rearrange("(a p) c -> p a c", p=128))
    p.dma("gpsimd", dpars[:, :], dpar[:, :])
    p.dma("gpsimd", creps[:, :], crep[:, :])
    p.dma("gpsimd", wst[0][:, :].rearrange("p (a c) -> p a c", c=128),
          wproj[:, 2048 + 0 * 128: 2048 + 1 * 128].rearrange("(a p) c -> p a c", p=128))
    p.barrier()

    tchunks = [(0, 512), (512, 512), (1024, 512), (1536, 512), (2048, 128)]

    for r in range(18):
        nxt = None
        if r + 1 < 18:
            nxt = 2048 + (r + 1) * 128
        elif r + 1 == 18:
            nxt = 0
        if nxt is not None:
            p.dma("gpsimd", wst[(r + 1) % 2][:, :].rearrange("p (a c) -> p a c", c=128),
                  wproj[:, nxt: nxt + 128].rearrange("(a p) c -> p a c", p=128))
        for ci, (c0, cn) in enumerate(tchunks[:4]):
            for k in range(NDT):
                p.op("tensor", lambda t, c0=c0, cn=cn, k=k, r=r: t.matmul(
                    aux.ps_f32[:, c0: c0 + cn],
                    wst[r % 2][:, k * 128: (k + 1) * 128],
                    XNH[:, k * HWD + c0: k * HWD + c0 + cn],
                    start=(k == 0), stop=(k == NDT - 1)))
        p.barrier()
        p.op("vector", lambda v, r=r: v.tensor_copy(prj[0][:, 0:2048], aux.ps_f32[:, 0:2048]))
        p.barrier()
        for k in range(NDT):
            p.op("tensor", lambda t, k=k, r=r: t.matmul(
                aux.ps_f32[:, 0:128],
                wst[r % 2][:, k * 128: (k + 1) * 128],
                XNH[:, k * HWD + 2048: k * HWD + 2176],
                start=(k == 0), stop=(k == NDT - 1)))
        p.barrier()
        p.op("vector", lambda v, r=r: v.tensor_copy(prj[0][:, 2048:2176], aux.ps_f32[:, 0:128]))
        p.barrier()
        p.op("vector", lambda v, r=r: v.tensor_scalar(
            cacc[0][:, :], prj[0][:, 61:61 + TPC], wc7s[:, r * 7: r * 7 + 1], None, ALU.mult))
        p.barrier()
        for tap in range(1, 7):
            p.op("vector", lambda v, r=r, tap=tap: v.scalar_tensor_tensor(
                cacc[tap % 2][:, :], prj[0][:, 61 + tap: 61 + tap + TPC],
                wc7s[:, r * 7 + tap: r * 7 + tap + 1],
                cacc[(tap + 1) % 2][:, :], ALU.mult, ALU.add))
            p.barrier()
        p.op("scalar", lambda s, r=r: s.activation(
            out=XBC[:, r * TPC: (r + 1) * TPC], in_=cacc[0][:, :], func=AF.Silu,
            bias=wcbs[:, r: r + 1]))
        p.barrier()

    for r in range(16):
        if r + 1 < 16:
            p.dma("gpsimd", wst[(18 + r + 1) % 2][:, :].rearrange("p (a c) -> p a c", c=128),
                  wproj[:, (r + 1) * 128: (r + 2) * 128].rearrange("(a p) c -> p a c", p=128))
        else:
            p.dma("gpsimd", wst[(18 + r + 1) % 2][:, 0:NDT * 64].rearrange("p (a c) -> p a c", c=64),
                  wproj[:, 4352: 4352 + 64].rearrange("(a p) c -> p a c", p=128))
        for ci, (c0, cn) in enumerate(tchunks[:4]):
            for k in range(NDT):
                p.op("tensor", lambda t, c0=c0, cn=cn, k=k, r=r: t.matmul(
                    aux.ps_f32[:, c0: c0 + cn],
                    wst[(18 + r) % 2][:, k * 128: (k + 1) * 128],
                    XNH[:, k * HWD + 64 + c0: k * HWD + 64 + c0 + cn],
                    start=(k == 0), stop=(k == NDT - 1)))
        p.barrier()
        p.op("scalar", lambda s, r=r: s.activation(
            out=zst[:, :], in_=aux.ps_f32[:, 0:2048], func=AF.Silu))
        p.barrier()
        p.dma("gpsimd", o_z[:, r * TPC: (r + 1) * TPC], zst[:, :])
        p.barrier()

    for ci, (c0, cn) in enumerate(tchunks[:4]):
        for k in range(NDT):
            p.op("tensor", lambda t, c0=c0, cn=cn, k=k: t.matmul(
                aux.ps_f32[0:64, c0: c0 + cn],
                wst[(18 + 16) % 2][:, k * 64: (k + 1) * 64],
                XNH[:, k * HWD + 64 + c0: k * HWD + 64 + c0 + cn],
                start=(k == 0), stop=(k == NDT - 1)))
    p.barrier()
    p.op("vector", lambda v: v.tensor_copy(dtr[:, :], aux.ps_f32[0:64, 0:2048]))
    p.barrier()
    p.op("scalar", lambda s: s.activation(out=dts[0][:, :], in_=dtr[:, :], func=AF.Abs,
                                          bias=dtbs[:, 0:1]))
    p.barrier()
    p.op("scalar", lambda s: s.activation(out=dts[0][:, :], in_=dts[0][:, :], func=AF.Exp, scale=-1.0))
    p.barrier()
    p.op("vector", lambda v: v.tensor_scalar_add(dts[0][:, :], dts[0][:, :], 1.0))
    p.barrier()
    p.op("scalar", lambda s: s.activation(out=dts[0][:, :], in_=dts[0][:, :], func=AF.Ln))
    p.barrier()
    p.op("scalar", lambda s: s.activation(out=dts[1][:, :], in_=dtr[:, :], func=AF.Relu,
                                          bias=dtbs[:, 0:1]))
    p.barrier()
    p.op("vector", lambda v: v.tensor_add(dtr[:, :], dts[0][:, :], dts[1][:, :]))
    p.barrier()
    p.dma("gpsimd", o_dt[:, :], dtr[:, :])

    for ci in range(4):
        for k in range(16):
            p.op("tensor", lambda t, ci=ci, k=k: t.matmul(
                aux.ps_f32[0:32, ci * 512: (ci + 1) * 512],
                fcds[:, k * 32: (k + 1) * 32],
                XBC[:, k * TPC + ci * 512: k * TPC + (ci + 1) * 512],
                start=(k == 0), stop=(k == 15)))
    p.barrier()
    p.op("vector", lambda v: v.tensor_scalar(
        fc_sb[:, :], aux.ps_f32[0:32, 0:2048], dpars[:, 0:1], None, ALU.add))
    p.barrier()
    p.op("vector", lambda v: v.tensor_copy(fc_bf[:, :], fc_sb[:, :]))
    p.barrier()
    for pt in range(16):
        for ci in range(4):
            p.op("tensor", lambda t, pt=pt, ci=ci: t.matmul(
                aux.ps_f32[:, ci * 512: (ci + 1) * 512],
                creps[:, pt * 128: (pt + 1) * 128],
                fc_bf[:, ci * 512: (ci + 1) * 512],
                start=True, stop=True))
        p.barrier()
        p.op("vector", lambda v, pt=pt: v.tensor_mul(
            skst[:, :], XBC[:, pt * TPC: (pt + 1) * TPC], aux.ps_f32[:, 0:2048]))
        p.barrier()
        p.dma("gpsimd", o_skip[:, pt * TPC: (pt + 1) * TPC], skst[:, :])
    p.barrier()
    p.dma("sync", o_x2[:, :], XBC[:, 0: 16 * TPC])
    p.dma("gpsimd", o_bc[:, :], XBC[:, 16 * TPC: 18 * TPC])
    p.dma("gpsimd", o_cedge[0:1, :].rearrange("a b -> b a"),
          XBC[64:128, 16 * TPC + TPC - 1: 16 * TPC + TPC])
    p.dma("gpsimd", o_cedge[1:2, :].rearrange("a b -> b a"),
          XBC[64:128, 17 * TPC: 17 * TPC + 1])
    p.barrier()
    p.finish()
    return p


def _mk_crep():
    rep = np.zeros((32, 2048), dtype=np.float32)
    for pt in range(16):
        for r in range(128):
            h = 2 * pt + r // 64
            rep[h, pt * 128 + r] = 1.0
    return rep


def _mk_cum(fw=True):
    m = np.zeros((128, 128), dtype=np.float32)
    for tp in range(128):
        for t in range(128):
            if (tp <= t) if fw else (tp >= t):
                m[tp, t] = 1.0
    return m


def _mk_gmask(fw=True):
    m = np.zeros((128, 128), dtype=np.float32)
    for s in range(128):
        for t in range(128):
            if (s <= t) if fw else (s >= t):
                m[s, t] = 1.0
    return m


def build_l1a2():
    p = Prog("l1a2")
    i_x2 = p.input("x2_fm", (128, 16 * TPC), BF16)
    i_bc = p.input("bc_fm", (128, 2 * TPC), BF16)
    i_dt = p.input("dt", (64, TPC))
    alog = p.input("w_alog", (64, 1))
    cum_fw = p.const("c_cumfw", _mk_cum(True))
    cum_bw = p.const("c_cumbw", _mk_cum(False))
    gm_fw = p.const("c_gmfw", _mk_gmask(True))
    gm_bw = p.const("c_gmbw", _mk_gmask(False))
    o_ydiag = p.output("ydiag", (2, 128, 16 * TPC), BF16)
    o_states = p.output("states", (2, NCH, 64, 2048), BF16)
    o_rloc = p.output("rloc", (2, NCH, 64, 2048), BF16)
    o_E = p.output("Ee", (2, 64, 2048))
    o_acs = p.output("acs", (2, NCH, TCH, 32))
    o_tzc = p.output("tzc", (1, 64 * 33))
    s_acs = p.scratch("s_acs", (2, NCH, TCH, 32))
    s_tzc = p.scratch("s_tzc", (64, 33))

    aux = Aux(p)
    BC = p.sb("BC", [128, 2 * TPC], BF16)
    dt_sb = p.sb("dt_sb", [64, TPC], F32)
    Ab = p.sb("Ab", [64, TPC], F32)
    negA = p.sb("negA", [64, 1], F32)
    AbT = p.sb("AbT", [128, 64], F32)
    dtT_all = p.sb("dtT_all", [128, 16 * 64], F32)
    acsT_all = p.sb("acsT_all", [128, 2 * 16 * 32], F32)
    TzS = p.sb("TzS", [64, 16], F32)
    Tzw = [p.sb(f"Tzw{i}", [64, 16], F32) for i in range(2)]
    TZC = p.sb("TZC", [64, 33], F32)
    tzrow = p.sb("tzrow", [1, 64 * 33], F32)

    acsflat = p.sb("acsflat", [1, TCH * 32], F32)
    srow = acsflat[0:1, 0:TPC]
    xsl = p.sb("xsl", [128, 2048], BF16)
    Xtm = p.sb("Xtm_c", [128, 2048], BF16)
    Xp = p.sb("Xp", [128, 2048], BF16)
    Gsb = p.sb("Gsb", [128, 128], BF16)
    Wf32 = [p.sb(f"Wf32_{i}", [128, 512], F32) for i in range(2)]
    Wexp = [p.sb(f"Wexp_{i}", [128, 512], BF16) for i in range(2)]
    Wsb = p.sb("Wsb", [128, 32 * 128], BF16)
    BT = p.sb("BT", [128, 64], BF16)
    Bds = p.sb("Bds", [128, 2048], BF16)
    dssb = p.sb("dssb", [128, 32], F32)
    dstmp = p.sb("dstmp", [128, 32], F32)
    ydst = p.sb("ydst", [128, 2048], BF16)
    Sstage = p.sb("Sstage", [64, 2048], BF16)
    Rbuf = [p.sb(f"Rbuf{i}", [64, 2048], F32) for i in range(2)]
    eTsb = p.sb("eTsb", [64, 2048], F32)

    Csb = p.sb("Csb", [64, 2 * TPC], BF16)
    p.start()
    aux.init_phase()
    p.dma("sync", BC[:, :], i_bc[:, :])
    p.dma("sync", Csb[:, :], i_bc[64:128, :])
    p.dma("sync", dt_sb[:, :], i_dt[:, :])
    p.dma("sync", negA[:, :], alog[:, :])
    cumfw_sb = p.sb("cumfw_sb", [128, 128], F32)
    cumbw_sb = p.sb("cumbw_sb", [128, 128], F32)
    gmfw_sb = p.sb("gmfw_sb", [128, 128], BF16)
    gmbw_sb = p.sb("gmbw_sb", [128, 128], BF16)
    p.dma("sync", cumfw_sb[:, :], cum_fw[:, :])
    p.dma("sync", cumbw_sb[:, :], cum_bw[:, :])
    p.dma("gpsimd", gmfw_sb[:, :], gm_fw[:, :])
    p.dma("gpsimd", gmbw_sb[:, :], gm_bw[:, :])
    p.barrier()
    p.op("scalar", lambda s: s.activation(out=negA[:, :], in_=negA[:, :], func=AF.Exp))
    p.barrier()
    p.op("vector", lambda v: v.tensor_scalar(negA[:, :], negA[:, :], -1.0, None, ALU.mult))
    p.barrier()
    p.op("vector", lambda v: v.tensor_scalar(Ab[:, :], dt_sb[:, :], negA[:, 0:1], None, ALU.mult))
    p.barrier()

    for sl in range(16):
        t0 = sl * TCH
        p.op("tensor", lambda t, t0=t0: t.transpose(
            aux.ps_f32[:, 0:64], Ab[:, t0:t0 + TCH], aux.ident_f32[0:64, 0:64]))
        p.op("tensor", lambda t, t0=t0: t.transpose(
            aux.ps_f32[:, 512:576], dt_sb[:, t0:t0 + TCH], aux.ident_f32[0:64, 0:64]))
        p.barrier()
        p.op("vector", lambda v: v.tensor_copy(AbT[:, :], aux.ps_f32[:, 0:64]))
        p.op("vector", lambda v, sl=sl: v.tensor_copy(
            dtT_all[:, sl * 64:(sl + 1) * 64], aux.ps_f32[:, 512:576]))
        p.barrier()
        for d in range(2):
            k = sl if d == 0 else 15 - sl
            cum = cumfw_sb if d == 0 else cumbw_sb
            p.op("tensor", lambda t, cum=cum, d=d: t.matmul(
                aux.ps_f32[:, 1024:1056], cum[:, :], AbT[:, d * 32:(d + 1) * 32],
                start=True, stop=True))
            p.op("vector", lambda v, d=d, sl=sl: v.reduce_sum(
                TzS[d * 32:(d + 1) * 32, (sl if d == 0 else 15 - sl):(sl if d == 0 else 15 - sl) + 1],
                Ab[d * 32:(d + 1) * 32, sl * TCH:(sl + 1) * TCH], axis=AX.X))
            p.barrier()
            p.op("vector", lambda v, d=d, k=k: v.tensor_copy(
                acsT_all[:, (d * 16 + k) * 32:(d * 16 + k + 1) * 32], aux.ps_f32[:, 1024:1056]))
            p.barrier()
            p.dma("gpsimd", s_acs[d, k, :, :], acsT_all[:, (d * 16 + k) * 32:(d * 16 + k + 1) * 32])
            p.dma("scalar", o_acs[d, k, :, :], acsT_all[:, (d * 16 + k) * 32:(d * 16 + k + 1) * 32])
    p.barrier()
    p.op("vector", lambda v: v.tensor_copy(Tzw[0][:, :], TzS[:, :]))
    p.barrier()
    for i, sh in enumerate([1, 2, 4, 8]):
        src, dst = Tzw[i % 2], Tzw[(i + 1) % 2]
        p.op("vector", lambda v, src=src, dst=dst, sh=sh: v.tensor_copy(dst[:, 0:sh], src[:, 0:sh]))
        p.barrier()
        p.op("vector", lambda v, src=src, dst=dst, sh=sh: v.tensor_add(
            dst[:, sh:16], src[:, sh:16], src[:, 0:16 - sh]))
        p.barrier()
    p.op("vector", lambda v: v.tensor_copy(TZC[:, 0:16], TzS[:, :]))
    p.op("vector", lambda v: v.memset(TZC[:, 16:17], 0.0))
    p.op("vector", lambda v: v.tensor_copy(TZC[:, 17:33], Tzw[0][:, 0:16]))
    p.barrier()
    p.dma("gpsimd", s_tzc[:, :], TZC[:, :])
    p.barrier()
    p.dma("sync", tzrow[0:1, :], s_tzc.rearrange("a b -> (a b)").unsqueeze(0))
    p.barrier()
    p.dma("scalar", o_tzc[0:1, :], tzrow[0:1, :])
    p.barrier()

    for d in range(2):
        gm = gmfw_sb if d == 0 else gmbw_sb
        p.op("vector", lambda v, d=d: v.memset(Rbuf[0][:, :], 0.0))
        p.barrier()
        for k in range(NCH):
            sl = k if d == 0 else 15 - k
            t0 = sl * TCH
            rb = Rbuf[k % 2]
            rbn = Rbuf[(k + 1) % 2]
            p.dma("sync", xsl[:, :].rearrange("p (a c) -> p a c", c=TCH),
                  i_x2.rearrange("p (a c) -> p a c", c=TPC)[:, :, t0:t0 + TCH])
            p.dma("scalar", acsflat[0:1, :], s_acs.rearrange("d k t h -> d k (t h)")[d, k].unsqueeze(0))
            p.dma("gpsimd", o_rloc[d, k, :, :], rb[:, :])
            p.barrier()
            for pt in range(16):
                p.op("tensor", lambda t, pt=pt: t.transpose(
                    aux.ps_bf[:, pt * 128:(pt + 1) * 128],
                    xsl[:, pt * 128:(pt + 1) * 128], aux.ident_bf[:, :]))
            p.op("tensor", lambda t, d=d, t0=t0: t.transpose(
                aux.ps_bf[:, 2048:2112], BC[0:64, d * TPC + t0: d * TPC + t0 + TCH],
                aux.ident_bf[0:64, 0:64]))
            p.op("tensor", lambda t, d=d, t0=t0: t.matmul(
                aux.ps_f32[:, 0:128], BC[0:64, d * TPC + t0: d * TPC + t0 + TCH],
                Csb[:, d * TPC + t0: d * TPC + t0 + TCH], start=True, stop=True))
            p.op("tensor", lambda t, d=d, k=k: t.matmul(
                aux.ps_f32[:, 512:544], aux.ones_row[0:1, :],
                tzrow[0:1, :].rearrange("a (h c) -> a h c", c=33)[:, d * 32:(d + 1) * 32, k:k + 1]
                .broadcast_to([1, 32, 1]),
                start=True, stop=True))
            p.barrier()
            p.op("vector", lambda v: v.tensor_copy(Xtm[:, :], aux.ps_bf[:, 0:2048]))
            p.op("vector", lambda v: v.tensor_copy(BT[:, :], aux.ps_bf[:, 2048:2112]))
            p.op("vector", lambda v, gm=gm: v.tensor_mul(Gsb[:, :], aux.ps_f32[:, 0:128], gm[:, :]))
            p.op("vector", lambda v, d=d, k=k: v.tensor_sub(
                dstmp[:, :], aux.ps_f32[:, 512:544],
                acsT_all[:, (d * 16 + k) * 32:(d * 16 + k + 1) * 32]))
            p.barrier()
            p.op("scalar", lambda s: s.activation(out=dssb[:, :], in_=dstmp[:, :], func=AF.Exp))
            p.op("vector", lambda v, d=d, sl=sl: v.tensor_mul(
                Xp[:, :].rearrange("p (h n) -> p h n", n=64),
                Xtm[:, :].rearrange("p (h n) -> p h n", n=64),
                dtT_all[:, sl * 64 + d * 32: sl * 64 + (d + 1) * 32]
                .unsqueeze(2).broadcast_to([128, 32, 64])))
            p.barrier()
            p.op("vector", lambda v: v.tensor_mul(
                Bds[:, :].rearrange("p (h n) -> p h n", n=64),
                BT[:, 0:64].unsqueeze(1).broadcast_to([128, 32, 64]),
                dssb[:, :].unsqueeze(2).broadcast_to([128, 32, 64])))
            p.barrier()
            NG = 8
            for g in range(NG + 3):
                if g < NG:
                    p.op("tensor", lambda t, g=g, d=d: t.matmul(
                        aux.ps_f32[:, 1024 + (g % 2) * 512: 1024 + (g % 2) * 512 + 512],
                        aux.ones_row[0:1, :],
                        acsflat[0:1, :].rearrange("a (t h) -> a t h", h=32)
                        [:, :, g * 4:(g + 1) * 4].transpose([0, 2, 1]),
                        start=True, stop=True))
                if 0 <= g - 1 < NG:
                    gg = g - 1
                    for hh in range(4):
                        p.op("vector", lambda v, gg=gg, hh=hh, d=d, k=k: v.tensor_scalar(
                            Wf32[gg % 2][:, hh * 128:(hh + 1) * 128],
                            aux.ps_f32[:, 1024 + (gg % 2) * 512 + hh * 128:
                                       1024 + (gg % 2) * 512 + (hh + 1) * 128],
                            acsT_all[:, (d * 16 + k) * 32 + gg * 4 + hh:
                                     (d * 16 + k) * 32 + gg * 4 + hh + 1],
                            0.0, ALU.subtract, ALU.min))
                if 0 <= g - 2 < NG:
                    gg = g - 2
                    p.op("scalar", lambda s, gg=gg: s.activation(
                        out=Wexp[gg % 2][:, :], in_=Wf32[gg % 2][:, :], func=AF.Exp))
                if 0 <= g - 3 < NG:
                    gg = g - 3
                    p.op("vector", lambda v, gg=gg: v.tensor_mul(
                        Wsb[:, gg * 512:(gg + 1) * 512].rearrange("p (h t) -> p h t", t=128),
                        Wexp[gg % 2][:, :].rearrange("p (h t) -> p h t", t=128),
                        Gsb[:, :].unsqueeze(1).broadcast_to([128, 4, 128])))
                p.barrier()
            for h in range(32):
                par = h % 2
                i = h // 2
                p.op("tensor", lambda t, h=h, i=i, par=par: t.matmul(
                    aux.ps_f32[par * 64:(par + 1) * 64, i * 128:(i + 1) * 128],
                    Xp[:, h * 64:(h + 1) * 64],
                    Wsb[:, h * 128:(h + 1) * 128], start=True, stop=True,
                    tile_position=(0, 64) if par else (0, 0)))
            p.barrier()
            p.op("vector", lambda v: v.tensor_copy(ydst[:, :], aux.ps_f32[:, 0:2048]))
            p.barrier()
            for h in range(32):
                p.op("tensor", lambda t, h=h: t.matmul(
                    aux.ps_f32[0:64, h * 64:(h + 1) * 64],
                    Bds[:, h * 64:(h + 1) * 64],
                    Xp[:, h * 64:(h + 1) * 64], start=True, stop=True))
            p.dma("gpsimd", o_ydiag.rearrange("d p (a c) -> d p a c", c=TPC)[d, :, :, t0:t0 + TCH],
                  ydst[:, :].rearrange("p (a c) -> p a c", c=TCH))
            p.barrier()
            p.op("vector", lambda v: v.tensor_copy(Sstage[:, :], aux.ps_f32[0:64, 0:2048]))
            p.barrier()
            for qq in range(4):
                p.op("tensor", lambda t, d=d, k=k, qq=qq: t.matmul(
                    aux.ps_f32[0:64, qq * 512:(qq + 1) * 512],
                    aux.ones_row[0:1, 0:64],
                    tzrow[0:1, :].rearrange("a (h c) -> a h c", c=33)
                    [:, d * 32 + qq * 8: d * 32 + (qq + 1) * 8, k:k + 1]
                    .broadcast_to([1, 8, 64]),
                    start=True, stop=True))
            p.dma("gpsimd", o_states[d, k, :, :], Sstage[:, :])
            p.barrier()
            p.op("scalar", lambda s: s.activation(out=eTsb[:, :], in_=aux.ps_f32[0:64, 0:2048], func=AF.Exp))
            p.barrier()
            p.op("vector", lambda v, rb=rb: v.tensor_mul(eTsb[:, :], eTsb[:, :], rb[:, :]))
            p.barrier()
            p.op("vector", lambda v, rbn=rbn: v.tensor_add(rbn[:, :], eTsb[:, :], Sstage[:, :]))
            p.barrier()
        p.dma("sync", o_E[d, :, :], Rbuf[16 % 2][:, :])
        p.barrier()
    p.finish()
    return p


def build_l1b():
    p = Prog("l1b")
    i_ydiag = p.input("ydiag", (2, 128, 16 * TPC), BF16)
    i_rloc = p.input("rloc", (2, NCH, 64, 2048), BF16)
    i_acs = p.input("acs", (2, NCH, TCH, 32))
    i_tzc = p.input("tzc", (1, 64 * 33))
    i_predE = p.input("predE", (2, 3, 64, 2048))
    i_predlam = p.input("predlam", (2, 3, 1, 64))
    i_cedge = p.input("cedge", (2, 64))
    i_bc = p.input("bc_fm", (128, 2 * TPC), BF16)
    i_skip = p.input("skip", (128, 16 * TPC), BF16)
    i_z = p.input("z_act", (128, 16 * TPC), BF16)
    w_normw = p.input("w_normw", (128, 16))
    w_opw = p.input("w_opw", (DI, D))
    o_attn = p.output("attn_fm", (128, NDT * TPC), BF16)

    aux = Aux(p, psum_mode="f32")
    YS = 2052
    Ybuf = [p.sb(f"Ybuf{i}", [128, 16 * YS], BF16) for i in range(2)]
    Rin = p.sb("Rin", [64, 2048], F32)
    Etmp = p.sb("Etmp", [64, 2048], BF16)
    lam = p.sb("lam", [64, 2048], F32)
    acc = p.sb("acc", [64, 2048], F32)
    Din = lam
    Rtru = acc
    rlc = p.sb("rlc", [64, 2048], BF16)
    Csb = p.sb("Csb", [64, 2 * TPC], BF16)
    EAt = p.sb("EAt", [64, 512], BF16)
    CEA = p.sb("CEA", [64, 512], BF16)
    acsflat = p.sb("acsflat", [1, TCH * 32], F32)
    srow = acsflat[0:1, 0:TPC]
    tzrow = p.sb("tzrow", [1, 64 * 33], F32)

    ced = p.sb("ced", [64, 2], F32)
    plam_sb = p.sb("plam_sb", [1, 2 * 3 * 64], F32)

    nw = p.sb("nw", [128, 16], F32)
    ztile = p.sb("ztile", [128, TPC], BF16)
    sqb = ztile

    sktile = p.sb("sktile", [128, TPC], BF16)
    opwt = None
    ost = None

    p.start()
    aux.init_phase()
    p.dma("sync", tzrow[0:1, :], i_tzc[0:1, :])
    p.dma("sync", Csb[:, :], i_bc[64:128, :])
    p.dma("sync", ced[:, 0:1], i_cedge[0].unsqueeze(1))
    p.dma("sync", ced[:, 1:2], i_cedge[1].unsqueeze(1))
    p.dma("sync", nw[:, :], w_normw[:, :])
    p.dma("sync", plam_sb[0:1, :], i_predlam.rearrange("a b c d -> (a b c d)").unsqueeze(0))
    p.barrier()

    for d in range(2):
        p.dma("gpsimd", Etmp[:, :], i_predE[d, 2, :, :])
        p.barrier()
        p.op("vector", lambda v: v.tensor_copy(acc[:, :], Etmp[:, :]))
        p.barrier()
        for j in (1, 0):
            for qq in range(4):
                p.op("tensor", lambda t, d=d, j=j, qq=qq: t.matmul(
                    aux.ps_f32[0:64, qq * 512:(qq + 1) * 512],
                    aux.ones_row[0:1, 0:64],
                    plam_sb[0:1, (d * 3 + j) * 64 + qq * 8: (d * 3 + j) * 64 + (qq + 1) * 8]
                    .unsqueeze(2).broadcast_to([1, 8, 64]),
                    start=True, stop=True))
            p.dma("gpsimd", Etmp[:, :], i_predE[d, j, :, :])
            p.barrier()
            p.op("scalar", lambda s: s.activation(out=lam[:, :], in_=aux.ps_f32[0:64, 0:2048], func=AF.Exp))
            p.barrier()
            p.op("vector", lambda v: v.tensor_mul(acc[:, :], acc[:, :], lam[:, :]))
            p.barrier()
            p.op("vector", lambda v: v.tensor_add(acc[:, :], acc[:, :], Etmp[:, :]))
            p.barrier()
        p.op("vector", lambda v: v.tensor_copy(Rin[:, :], acc[:, :]))
        p.barrier()
        off = 1 if d == 0 else 0
        p.dma("sync", Ybuf[d][:, :].rearrange("p (a c) -> p a c", c=YS)[:, :, off:off + TPC],
              i_ydiag.rearrange("d p (a c) -> d p a c", c=TPC)[d])
        p.barrier()
        hcol = 0 if d == 0 else TPC
        for j in range(16):
            p.op("tensor", lambda t, d=d, j=j: t.matmul(
                aux.ps_f32[:, 3072 + j: 3072 + j + 1],
                Rin[:, j * 128:(j + 1) * 128],
                ced[:, d:d + 1], start=True, stop=True))
        p.barrier()
        for j in range(16):
            p.op("vector", lambda v, d=d, j=j, hcol=hcol: v.tensor_copy(
                Ybuf[d][:, j * YS + hcol: j * YS + hcol + 1],
                aux.ps_f32[:, 3072 + j: 3072 + j + 1]))
        p.barrier()
        for k in range(NCH):
            sl = k if d == 0 else 15 - k
            t0 = sl * TCH
            p.dma("scalar", acsflat[0:1, :], i_acs.rearrange("d k t h -> d k (t h)")[d, k].unsqueeze(0))
            p.dma("sync", rlc[:, :], i_rloc[d, k, :, :])
            p.barrier()
            for qq in range(4):
                p.op("tensor", lambda t, d=d, k=k, qq=qq: t.matmul(
                    aux.ps_f32[0:64, 2048 + qq * 512: 2048 + (qq + 1) * 512],
                    aux.ones_row[0:1, 0:64],
                    tzrow[0:1, :].rearrange("a (h c) -> a h c", c=33)
                    [:, d * 32 + qq * 8: d * 32 + (qq + 1) * 8, 16 + k:16 + k + 1]
                    .broadcast_to([1, 8, 64]),
                    start=True, stop=True))
            p.barrier()
            p.op("scalar", lambda s: s.activation(out=Din[:, :], in_=aux.ps_f32[0:64, 2048:4096], func=AF.Exp))
            p.barrier()
            p.op("vector", lambda v: v.tensor_mul(Din[:, :], Din[:, :], Rin[:, :]))
            p.barrier()
            p.op("vector", lambda v: v.tensor_add(Rtru[:, :], Din[:, :], rlc[:, :]))
            p.barrier()
            p.op("vector", lambda v: v.tensor_copy(Etmp[:, :], Rtru[:, :]))
            p.barrier()
            for g in range(8):
                p.op("tensor", lambda t, g=g: t.matmul(
                    aux.ps_f32[0:64, 2048 + (g % 2) * 512: 2048 + (g % 2) * 512 + 512],
                    aux.ones_row[0:1, 0:64],
                    acsflat[0:1, :].rearrange("a (t h) -> a t h", h=32)
                    [:, :, g * 4:(g + 1) * 4].transpose([0, 2, 1]),
                    start=True, stop=True))
                p.barrier()
                p.op("scalar", lambda s, g=g: s.activation(
                    out=EAt[:, :], in_=aux.ps_f32[0:64, 2048 + (g % 2) * 512: 2048 + (g % 2) * 512 + 512],
                    func=AF.Exp))
                p.barrier()
                p.op("vector", lambda v, d=d, t0=t0: v.tensor_mul(
                    CEA[:, :].rearrange("p (h t) -> p h t", t=128),
                    Csb[:, d * TPC + t0: d * TPC + t0 + TCH].unsqueeze(1).broadcast_to([64, 4, 128]),
                    EAt[:, :].rearrange("p (h t) -> p h t", t=128)))
                p.barrier()
                for hh in range(4):
                    h = g * 4 + hh
                    par = h % 2
                    i = h // 2
                    p.op("tensor", lambda t, h=h, hh=hh, par=par, i=i: t.matmul(
                        aux.ps_f32[par * 64:(par + 1) * 64, i * 128:(i + 1) * 128],
                        Etmp[:, h * 64:(h + 1) * 64],
                        CEA[:, hh * 128:(hh + 1) * 128], start=True, stop=True,
                        tile_position=(0, 64) if par else (0, 0)))
                p.barrier()
            p.op("vector", lambda v, d=d, t0=t0, off=(1 if d == 0 else 0): v.tensor_tensor(
                Ybuf[d][:, :].rearrange("p (a c) -> p a c", c=YS)[:, :, off + t0: off + t0 + TCH],
                Ybuf[d][:, :].rearrange("p (a c) -> p a c", c=YS)[:, :, off + t0: off + t0 + TCH],
                aux.ps_f32[:, 0:2048].rearrange("p (a c) -> p a c", c=TCH),
                ALU.add))
            p.barrier()

    for j in range(16):
        p.dma("sync", sktile[:, :], i_skip[:, j * TPC:(j + 1) * TPC])
        p.barrier()
        p.op("vector", lambda v, j=j: v.tensor_add(
            sktile[:, :], sktile[:, :], Ybuf[0][:, j * YS: j * YS + TPC]))
        p.barrier()
        p.op("vector", lambda v, j=j: v.tensor_add(
            sktile[:, :], sktile[:, :], Ybuf[1][:, j * YS + 1: j * YS + 1 + TPC]))
        p.barrier()
        p.op("vector", lambda v, j=j: v.tensor_copy(Ybuf[0][:, j * YS: j * YS + TPC], sktile[:, :]))
        p.op("scalar", lambda s, j=j: s.activation(out=sqb[:, :], in_=sktile[:, :], func=AF.Square))
        p.barrier()
        for ci in range(4):
            p.op("tensor", lambda t, j=j, ci=ci: t.matmul(
                aux.ps_f32[0:1, 2048 + ci * 512: 2048 + ci * 512 + 512],
                aux.ones_col_bf[:, 0:1],
                sqb[:, ci * 512:(ci + 1) * 512],
                start=(j == 0), stop=(j == 15)))
        p.barrier()
    p.op("vector", lambda v: v.tensor_copy(srow, aux.ps_f32[0:1, 2048:4096]))
    p.barrier()
    p.op("vector", lambda v: v.tensor_scalar(srow, srow, 1.0 / 2048.0, 1e-5, ALU.mult, ALU.add))
    p.barrier()
    p.op("scalar", lambda s: s.activation(out=srow, in_=srow, func=AF.Ln))
    p.barrier()
    p.op("scalar", lambda s: s.activation(out=srow, in_=srow, func=AF.Exp, scale=-0.5))
    p.barrier()
    for ci in range(4):
        p.op("tensor", lambda t, ci=ci: t.matmul(
            aux.ps_f32[:, ci * 512:(ci + 1) * 512], aux.ones_row[0:1, :],
            srow[0:1, ci * 512:(ci + 1) * 512], start=True, stop=True))
    p.barrier()
    for j in range(16):
        p.dma("sync", ztile[:, :], i_z[:, j * TPC:(j + 1) * TPC])
        p.barrier()
        p.op("vector", lambda v, j=j: v.tensor_mul(
            sktile[:, :], Ybuf[0][:, j * YS: j * YS + TPC], aux.ps_f32[:, 0:2048]))
        p.barrier()
        p.op("vector", lambda v, j=j: v.tensor_scalar(
            sktile[:, :], sktile[:, :], nw[:, j:j + 1], None, ALU.mult))
        p.barrier()
        p.op("vector", lambda v, j=j: v.tensor_mul(
            Ybuf[0][:, j * YS: j * YS + TPC], sktile[:, :], ztile[:, :]))
        p.barrier()
    opwt = sktile
    for dsl in range(8):
        p.dma("gpsimd", opwt[:, :].rearrange("p (a c) -> p a c", c=128),
              w_opw[:, dsl * 128:(dsl + 1) * 128].rearrange("(a p) c -> p a c", p=128))
        p.barrier()
        for tch in range(4):
            for kt in range(16):
                p.op("tensor", lambda t, dsl=dsl, tch=tch, kt=kt: t.matmul(
                    aux.ps_f32[:, tch * 512:(tch + 1) * 512],
                    opwt[:, kt * 128:(kt + 1) * 128],
                    Ybuf[0][:, kt * YS + tch * 512: kt * YS + tch * 512 + 512],
                    start=(kt == 0), stop=(kt == 15)))
        p.barrier()
        for tch in range(4):
            p.op("vector", lambda v, tch=tch: v.tensor_copy(
                ztile[:, tch * 512:(tch + 1) * 512], aux.ps_f32[:, tch * 512:(tch + 1) * 512]))
        p.barrier()
        p.dma("scalar", o_attn[:, dsl * TPC: (dsl + 1) * TPC], ztile[:, :])
        p.barrier()
    p.finish()
    return p


def host_l1_attn(xn1_fm, W):
    import ml_dtypes
    BFD = ml_dtypes.bfloat16
    p1 = _get_prog("l1a1", build_l1a1)
    p2 = _get_prog("l1a2", build_l1a2)
    p3 = _get_prog("l1b", build_l1b)

    wm1 = {
        "w_proj": np.ascontiguousarray(W["in_proj_w"].T),
        "w_conv": np.ascontiguousarray(
            W["conv_w"].reshape(2304, 7).reshape(18, 128, 7).transpose(1, 0, 2).reshape(128, 126)),
        "w_convb": np.ascontiguousarray(W["conv_b"].reshape(18, 128).T),
        "w_dtb": np.concatenate([W["dt_bias"], W["dt_bias"]]).reshape(64, 1).astype(np.float32),
        "w_alog": np.concatenate([W["A_log"], W["A_log"]]).reshape(64, 1).astype(np.float32),
        "w_fcd": np.ascontiguousarray(W["fc_D_w"].T),
        "w_dpar": W["D_param"].reshape(32, 1).astype(np.float32),
    }
    zeros64 = np.zeros((128, 64), BFD)
    maps1 = []
    for c in range(NCORE):
        b, s = c // 4, c % 4
        prv = xn1_fm[c - 1] if s > 0 else None
        nxt = xn1_fm[c + 1] if s < 3 else None
        cols = []
        for di in range(NDT):
            lo = prv[:, di * TPC + TPC - 64: (di + 1) * TPC] if prv is not None else zeros64
            hi = nxt[:, di * TPC: di * TPC + 64] if nxt is not None else zeros64
            cols.append(np.concatenate([lo, xn1_fm[c][:, di * TPC:(di + 1) * TPC], hi], axis=1))
        xnh = np.concatenate(cols, axis=1)
        maps1.append({"xn1h": np.ascontiguousarray(xnh), **wm1})
    r1 = p1.run(maps1)

    maps2 = [{"x2_fm": np.asarray(r1[c]["x2_fm"]), "bc_fm": np.asarray(r1[c]["bc_fm"]),
              "dt": np.asarray(r1[c]["dt"]), "w_alog": wm1["w_alog"]} for c in range(NCORE)]
    r2 = p2.run(maps2)

    maps3 = []
    for c in range(NCORE):
        b, s = c // 4, c % 4
        predE = np.zeros((2, 3, 64, 2048), np.float32)
        predlam = np.zeros((2, 3, 1, 64), np.float32)
        for j in range(3):
            cf = c - 1 - j if s - 1 - j >= 0 else None
            if cf is not None:
                predE[0, j] = np.asarray(r2[cf]["Ee"])[0]
                tz = np.asarray(r2[cf]["tzc"]).reshape(64, 33)
                predlam[0, j, 0, 0:32] = tz[0:32, 32]
            cbk = c + 1 + j if s + 1 + j <= 3 else None
            if cbk is not None:
                predE[1, j] = np.asarray(r2[cbk]["Ee"])[1]
                tz = np.asarray(r2[cbk]["tzc"]).reshape(64, 33)
                predlam[1, j, 0, 0:32] = tz[32:64, 32]
        cedge = np.zeros((2, 64), np.float32)
        if s > 0:
            cedge[0] = np.asarray(r1[c - 1]["c_edge"])[0]
        if s < 3:
            cedge[1] = np.asarray(r1[c + 1]["c_edge"])[1]
        maps3.append({
            "ydiag": np.asarray(r2[c]["ydiag"]), "rloc": np.asarray(r2[c]["rloc"]),
            "acs": np.asarray(r2[c]["acs"]), "tzc": np.asarray(r2[c]["tzc"]),
            "predE": predE, "predlam": predlam, "cedge": cedge,
            "bc_fm": np.asarray(r1[c]["bc_fm"]),
            "skip": np.asarray(r1[c]["skip"]), "z_act": np.asarray(r1[c]["z_act"]),
            "w_normw": np.ascontiguousarray(W["norm_w"].reshape(16, 128).T).astype(np.float32),
            "w_opw": np.ascontiguousarray(W["out_proj_w"].T),
        })
    r3 = p3.run(maps3)
    return [np.asarray(r3[c]["attn_fm"]) for c in range(NCORE)]



KLW = 3072
QLW = 2176
QOF = 896
XLW = 3200
NQB = 17


def _mk_rot128():
    R = np.zeros((128, 128), np.float32)
    for blk in range(2):
        o = blk * 64
        for dk in range(32):
            R[o + dk + 32, o + dk] = -1.0
            R[o + dk, o + dk + 32] = 1.0
    return R


def _mk_bd2():
    m = np.zeros((128, 2), np.float32)
    m[0:64, 0] = 1.0
    m[64:128, 1] = 1.0
    return m


def _mk_ind2():
    m = np.zeros((2, 128), np.float32)
    m[0, 0:64] = 1.0
    m[1, 64:128] = 1.0
    return m


def build_l1f():
    p = Prog("l1f")
    x_in = p.input("x_l1f", (128, NDT * XLW))
    wqkv = p.input("w_qkv", (D, 1536))
    wo_in = p.input("w_wo", (512, D))
    w1 = p.input("w_w1", (D, 5632))
    w2 = p.input("w_w2", (2816, D))
    b1a = p.input("w_b1a", (128, 22))
    b1g = p.input("w_b1g", (128, 22))
    b2c = p.input("w_b2", (128, NDT))
    lng = p.input("w_lng", (128, NDT))
    lnb = p.input("w_lnb", (128, NDT))
    qs_c = p.input("w_qs", (128, 1))
    ks_c = p.input("w_ks", (128, 1))
    rope = p.input("rope", (4, 128, KLW))
    masks = p.input("masks", (NQB, 128, 1024), BF16)
    res_a = p.input("res_a", (128, NDT * TPC))
    res_b = p.input("res_b", (128, NDT * TPC), BF16)
    o_out = p.output("xl_out", (128, NDT * TPC))

    aux = Aux(p)
    AR = p.sb("arena", [128, 82944], BF16)
    xbf = AR[:, 0:25600]
    qf = AR[:, 25600:37888]
    kf = AR[:, 37888:50176]
    vtm = AR[:, 50176:62464]
    tabs = AR[:, 62464:74752]
    o_fm = AR[:, 62464:71168]
    mk_st = [AR[:, 71168 + i * 1024: 72192 + i * 1024] for i in range(2)]
    Pb = AR[:, 73216:75264]
    PT = AR[:, 75264:77312]
    sqst = AR[:, 77312:80384]
    u_bf = AR[:, 0:45056]
    xh = AR[:, 45056:61440]
    xmid = AR[:, 61440:78848]
    xmid_e = AR[:, 25600:43008]
    atmp = AR[:, 78848:80896]
    gtmp = AR[:, 80896:82944]
    wstr = [p.sb(f"l1f_w{i}", [128, 2816], BF16) for i in range(2)]
    Smf = p.sb("Smf", [128, 2048], F32)
    srow = Smf[0:1, 0:2048]
    srow2 = p.sb("l1f_srow2", [1, 2048], BF16)[:, :]
    nmx = p.sb("nmx", [128, 2], F32)
    rsum = p.sb("rsum", [128, 2], F32)
    rows2 = p.sb("rows2", [2, KLW], BF16)
    msq = rows2[0:1, 0:TPC]
    t32 = [p.sb(f"t32_{i}", [128, 512], F32) for i in range(2)]
    colsA = p.sb("colsA", [128, 22], F32)
    colsG = p.sb("colsG", [128, 22], F32)
    colsM = p.sb("colsM", [128, 3 * NDT], F32)
    rot_sb = p.sb("rot_sb", [128, 128], BF16)
    bd2_sb = p.sb("bd2_sb", [128, 2], BF16)
    ind2_sb = p.sb("ind2_sb", [2, 128], BF16)
    qs_sb = p.sb("qs_sb", [128, 2], F32)
    resa_t = [p.sb(f"resa_t{i}", [128, 512], F32) for i in range(2)]
    resb_t = [p.sb(f"resb_t{i}", [128, 512], BF16) for i in range(2)]
    rot_c = p.const("c_rot", _mk_rot128())
    bd2_c = p.const("c_bd2", _mk_bd2())
    ind2_c = p.const("c_ind2", _mk_ind2())

    CH6 = [i * 512 for i in range(6)]
    CH4 = [i * 512 for i in range(4)]

    p.start()
    aux.init_phase()
    p.dma("gpsimd", xbf.rearrange("p (a c) -> p a c", c=XLW),
          x_in.rearrange("p (a c) -> p a c", c=XLW))
    p.dma("gpsimd", rot_sb[:, :], rot_c[:, :])
    p.dma("gpsimd", bd2_sb[:, :], bd2_c[:, :])
    p.dma("gpsimd", ind2_sb[:, :], ind2_c[:, :])
    p.dma("gpsimd", tabs.rearrange("p (a c) -> p a c", c=KLW), rope.transpose([1, 0, 2]))
    p.dma("sync", qs_sb[:, 0:1], qs_c[:, :])
    p.dma("sync", qs_sb[:, 1:2], ks_c[:, :])
    p.dma("scalar", colsA[:, :], b1a[:, :])
    p.dma("scalar", colsG[:, :], b1g[:, :])
    p.dma("scalar", colsM[:, 0:NDT], b2c[:, :])
    p.dma("scalar", colsM[:, NDT:2 * NDT], lng[:, :])
    p.dma("scalar", colsM[:, 2 * NDT:3 * NDT], lnb[:, :])
    p.barrier()

    def xsh(di, c0, cn):
        off = 128 + c0 - (1 if di >= 4 else 0)
        return xbf[:, di * XLW + off: di * XLW + off + cn]

    p.dma("gpsimd", wstr[0][:, 0:NDT * 128].rearrange("p (a c) -> p a c", c=128),
          wqkv[:, 0:128].rearrange("(a p) c -> p a c", p=128))
    p.barrier()
    prev = None
    for gidx in range(8):
        qk, slc = divmod(gidx, 4)
        dst = qf if qk == 0 else kf
        if gidx + 1 < 8:
            nq, ns = divmod(gidx + 1, 4)
            p.dma("gpsimd", wstr[(gidx + 1) % 2][:, 0:NDT * 128].rearrange("p (a c) -> p a c", c=128),
                  wqkv[:, nq * 512 + ns * 128: nq * 512 + (ns + 1) * 128]
                  .rearrange("(a p) c -> p a c", p=128))
        for ci, c0 in enumerate(CH6):
            bank = (ci % 2) * 512
            for k in range(NDT):
                p.op("tensor", lambda t, k=k, c0=c0, gidx=gidx, bank=bank: t.matmul(
                    aux.ps_f32[:, bank: bank + 512],
                    wstr[gidx % 2][:, k * 128:(k + 1) * 128],
                    xsh(k, c0, 512),
                    start=(k == 0), stop=(k == NDT - 1)))
            if prev is not None:
                pd, ps_, pc0, pb = prev
                p.op("vector", lambda v, pd=pd, ps_=ps_, pc0=pc0, pb=pb: v.tensor_copy(
                    pd[:, ps_ * KLW + pc0: ps_ * KLW + pc0 + 512],
                    aux.ps_f32[:, pb: pb + 512]))
            p.barrier()
            prev = (dst, slc, c0, bank)
    pd, ps_, pc0, pb = prev
    p.op("vector", lambda v, pd=pd, ps_=ps_, pc0=pc0, pb=pb: v.tensor_copy(
        pd[:, ps_ * KLW + pc0: ps_ * KLW + pc0 + 512], aux.ps_f32[:, pb: pb + 512]))
    p.barrier()

    p.dma("gpsimd", wstr[0][:, 0:2048].rearrange("p (a c) -> p a c", c=512),
          wqkv[0:512, 1024:1536].rearrange("(a p) c -> p a c", p=128))
    p.dma("gpsimd", wstr[1][:, 0:2048].rearrange("p (a c) -> p a c", c=512),
          wqkv[512:1024, 1024:1536].rearrange("(a p) c -> p a c", p=128))
    p.barrier()

    def vw_sl(k):
        return wstr[k // 4][:, (k % 4) * 512:(k % 4 + 1) * 512]

    for kb in range(25):
        if kb < 24:
            for k in range(NDT):
                p.op("tensor", lambda t, kb=kb, k=k: t.matmul(
                    aux.ps_f32[:, (kb % 2) * 512: (kb % 2) * 512 + 512],
                    xsh(k, kb * 128, 128),
                    vw_sl(k),
                    start=(k == 0), stop=(k == NDT - 1)))
        if kb > 0:
            p.op("vector", lambda v, kb=kb: v.tensor_copy(
                vtm[:, (kb - 1) * 512: kb * 512],
                aux.ps_f32[:, ((kb - 1) % 2) * 512: ((kb - 1) % 2) * 512 + 512]))
        p.barrier()

    for qk in range(2):
        src = qf if qk == 0 else kf
        cos_t = tabs[:, (0 if qk == 0 else 2) * KLW: (0 if qk == 0 else 2) * KLW + KLW]
        sin_t = tabs[:, (1 if qk == 0 else 3) * KLW: (1 if qk == 0 else 3) * KLW + KLW]
        for tile in range(4):
            sl = src[:, tile * KLW: (tile + 1) * KLW]
            p.op("scalar", lambda s, sl=sl: s.activation(out=sqst[:, 0:KLW], in_=sl, func=AF.Square))
            p.barrier()
            for ci, c0 in enumerate(CH6):
                p.op("tensor", lambda t, c0=c0, ci=ci: t.matmul(
                    aux.ps_f32[0:2, (ci % 2) * 512: (ci % 2) * 512 + 512],
                    bd2_sb[:, :], sqst[:, c0: c0 + 512], start=True, stop=True))
                if ci > 0:
                    p.op("vector", lambda v, ci=ci: v.tensor_copy(
                        rows2[:, (ci - 1) * 512: ci * 512],
                        aux.ps_f32[0:2, ((ci - 1) % 2) * 512: ((ci - 1) % 2) * 512 + 512]))
                p.barrier()
            p.op("vector", lambda v: v.tensor_copy(
                rows2[:, 5 * 512: 6 * 512], aux.ps_f32[0:2, (5 % 2) * 512: (5 % 2) * 512 + 512]))
            p.barrier()
            p.op("scalar", lambda s: s.activation(out=rows2[:, :], in_=rows2[:, :], func=AF.Sqrt))
            p.barrier()
            p.op("vector", lambda v: v.tensor_scalar(rows2[:, :], rows2[:, :], 1e-12, None, ALU.max))
            p.barrier()
            def _recip_bf(v):
                with p.nc.allow_low_precision(reason="l2norm rnorm in bf16"):
                    return v.reciprocal(rows2[:, :], rows2[:, :])
            p.op("vector", _recip_bf)
            p.barrier()
            for ci, c0 in enumerate(CH6[:4]):
                p.op("tensor", lambda t, c0=c0, ci=ci: t.matmul(
                    aux.ps_f32[:, ci * 512: ci * 512 + 512],
                    ind2_sb[:, :], rows2[:, c0: c0 + 512], start=True, stop=True))
            p.barrier()
            for ci, c0 in enumerate(CH6[:4]):
                p.op("vector", lambda v, sl=sl, c0=c0, ci=ci: v.tensor_mul(
                    sl[:, c0:c0 + 512], sl[:, c0:c0 + 512], aux.ps_f32[:, ci * 512: ci * 512 + 512]))
            p.barrier()
            for ci, c0 in enumerate(CH6[4:]):
                p.op("tensor", lambda t, c0=c0, ci=ci: t.matmul(
                    aux.ps_f32[:, ci * 512: ci * 512 + 512],
                    ind2_sb[:, :], rows2[:, c0: c0 + 512], start=True, stop=True))
            p.barrier()
            for ci, c0 in enumerate(CH6[4:]):
                p.op("vector", lambda v, sl=sl, c0=c0, ci=ci: v.tensor_mul(
                    sl[:, c0:c0 + 512], sl[:, c0:c0 + 512], aux.ps_f32[:, ci * 512: ci * 512 + 512]))
            p.barrier()
            if qk == 0:
                p.op("vector", lambda v, sl=sl: v.tensor_scalar(
                    sl[:, :], sl[:, :], qs_sb[:, 0:1], 8.0, ALU.mult, ALU.mult))
            else:
                p.op("vector", lambda v, sl=sl: v.tensor_scalar(
                    sl[:, :], sl[:, :], qs_sb[:, 1:2], None, ALU.mult))
            p.barrier()
            for ci, c0 in enumerate(CH6):
                bank = (ci % 2) * 512
                p.op("tensor", lambda t, sl=sl, c0=c0, bank=bank: t.matmul(
                    aux.ps_f32[:, bank: bank + 512],
                    rot_sb[:, :], sl[:, c0:c0 + 512], start=True, stop=True))
                p.barrier()
                p.op("vector", lambda v, c0=c0, bank=bank, sin_t=sin_t: v.tensor_mul(
                    t32[0][:, :], aux.ps_f32[:, bank: bank + 512], sin_t[:, c0:c0 + 512]))
                p.op("scalar", lambda s: s.engine_nop()) if False else None
                p.barrier()
                p.op("vector", lambda v, sl=sl, c0=c0, cos_t=cos_t: v.tensor_mul(
                    t32[1][:, :], sl[:, c0:c0 + 512], cos_t[:, c0:c0 + 512]))
                p.barrier()
                p.op("vector", lambda v, sl=sl, c0=c0: v.tensor_add(
                    sl[:, c0:c0 + 512], t32[0][:, :], t32[1][:, :]))
                p.barrier()

    for qb in range(NQB):
        kls = 512 * ((qb - 1) // 4) + 512
        p.dma("sync", mk_st[qb % 2][:, :], masks[qb, :, :])
        p.barrier()
        for tile in range(4):
            ql = QOF + qb * 128
            for hb in range(2):
                base = hb * 64
                for jj in range(2):
                    p.op("tensor", lambda t, tile=tile, hb=hb, jj=jj, ql=ql, kls=kls, base=base: t.matmul(
                        aux.ps_f32[:, hb * 1024 + jj * 512: hb * 1024 + jj * 512 + 512],
                        qf[base:base + 64, tile * KLW + ql: tile * KLW + ql + 128],
                        kf[base:base + 64, tile * KLW + kls + jj * 512: tile * KLW + kls + (jj + 1) * 512],
                        start=True, stop=True))
            p.barrier()
            p.op("vector", lambda v, qb=qb: v.tensor_tensor(
                Smf[:, :].rearrange("p (h c) -> p h c", c=1024),
                aux.ps_f32[:, 0:2048].rearrange("p (h c) -> p h c", c=1024),
                mk_st[qb % 2].unsqueeze(1).broadcast_to([128, 2, 1024]),
                ALU.add))
            p.barrier()
            for hb in range(2):
                p.op("vector", lambda v, hb=hb: v.reduce_max(
                    nmx[:, hb:hb + 1], Smf[:, hb * 1024:(hb + 1) * 1024], axis=AX.X, negate=True))
            p.barrier()
            for hb in range(2):
                p.op("scalar", lambda s, hb=hb: s.activation(
                    out=Pb[:, hb * 1024:(hb + 1) * 1024], in_=Smf[:, hb * 1024:(hb + 1) * 1024],
                    func=AF.Exp, bias=nmx[:, hb:hb + 1], accum_out=rsum[:, hb:hb + 1]))
            p.barrier()
            p.op("vector", lambda v: v.reciprocal(rsum[:, :], rsum[:, :]))
            p.barrier()
            for hb in range(2):
                p.op("vector", lambda v, hb=hb: v.tensor_scalar(
                    Pb[:, hb * 1024:(hb + 1) * 1024], Pb[:, hb * 1024:(hb + 1) * 1024],
                    rsum[:, hb:hb + 1], None, ALU.mult))
            p.barrier()
            for pc in range(16):
                p.op("tensor", lambda t, pc=pc: t.transpose(
                    aux.ps_tr[pc // 8][:, (pc % 8) * 128:(pc % 8 + 1) * 128],
                    Pb[:, pc * 128:(pc + 1) * 128], aux.ident_bf[:, :]))
            p.barrier()
            p.op("vector", lambda v: v.tensor_copy(PT[:, 0:1024], aux.ps_tr[0][:, :]))
            p.op("vector", lambda v: v.tensor_copy(PT[:, 1024:2048], aux.ps_tr[1][:, :]))
            p.barrier()
            for hb in range(2):
                h = tile * 2 + hb
                for j in range(8):
                    kb_abs = kls // 128 + j
                    p.op("tensor", lambda t, hb=hb, h=h, j=j, kb_abs=kb_abs: t.matmul(
                        aux.ps_f32[hb * 64:(hb + 1) * 64, hb * 512: hb * 512 + 128],
                        vtm[:, kb_abs * 512 + h * 64: kb_abs * 512 + (h + 1) * 64],
                        PT[:, hb * 1024 + j * 128: hb * 1024 + (j + 1) * 128],
                        start=(j == 0), stop=(j == 7),
                        tile_position=(0, 64) if hb else (0, 0)))
            p.barrier()
            p.op("vector", lambda v, tile=tile, qb=qb: v.tensor_copy(
                o_fm[0:64, tile * QLW + qb * 128: tile * QLW + (qb + 1) * 128],
                aux.ps_f32[0:64, 0:128]))
            p.op("vector", lambda v, tile=tile, qb=qb: v.tensor_copy(
                o_fm[64:128, tile * QLW + qb * 128: tile * QLW + (qb + 1) * 128],
                aux.ps_f32[64:128, 512:640]))
            p.barrier()

    CHQ = [(0, 512), (512, 512), (1024, 512), (1536, 512), (2048, 128)]
    for di in range(NDT):
        p.dma("gpsimd", wstr[di % 2][:, 0:512].rearrange("p (a c) -> p a c", c=128),
              wo_in[:, di * 128:(di + 1) * 128].rearrange("(a p) c -> p a c", p=128))
        p.barrier()
        for ci, (c0, cn) in enumerate(CHQ):
            for k in range(4):
                p.op("tensor", lambda t, di=di, c0=c0, cn=cn, k=k, ci=ci: t.matmul(
                    aux.ps_f32[:, (ci % 2) * 512: (ci % 2) * 512 + cn],
                    wstr[di % 2][:, k * 128: (k + 1) * 128],
                    o_fm[:, k * QLW + c0: k * QLW + c0 + cn],
                    start=(k == 0), stop=(k == 3)))
            p.barrier()
            p.op("vector", lambda v, di=di, c0=c0, cn=cn, ci=ci: v.tensor_add(
                xmid_e[:, di * QLW + c0: di * QLW + c0 + cn],
                xbf[:, di * XLW + 1024 + c0: di * XLW + 1024 + c0 + cn],
                aux.ps_f32[:, (ci % 2) * 512: (ci % 2) * 512 + cn]))
            p.barrier()

    p.op("vector", lambda v: v.tensor_copy(xmid, xmid_e))
    p.barrier()
    def fsh(di):
        return 128 - (1 if di >= 4 else 0)

    for di in range(NDT):
        p.op("scalar", lambda s, di=di: s.activation(
            out=u_bf[:, di * TPC:(di + 1) * TPC],
            in_=xmid[:, di * QLW + fsh(di): di * QLW + fsh(di) + TPC], func=AF.Square))
    p.barrier()
    for ci, c0 in enumerate(CH4):
        for di in range(NDT):
            p.op("tensor", lambda t, di=di, c0=c0: t.matmul(
                aux.ps_f32[0:1, c0:c0 + 512], aux.ones_col_bf[:, 0:1],
                xmid[:, di * QLW + fsh(di) + c0: di * QLW + fsh(di) + c0 + 512],
                start=(di == 0), stop=(di == NDT - 1)))
    p.barrier()
    p.op("vector", lambda v: v.tensor_copy(srow, aux.ps_f32[0:1, 0:2048]))
    p.barrier()
    for ci, c0 in enumerate(CH4):
        for di in range(NDT):
            p.op("tensor", lambda t, di=di, c0=c0: t.matmul(
                aux.ps_f32[0:1, c0:c0 + 512], aux.ones_col_bf[:, 0:1],
                u_bf[:, di * TPC + c0: di * TPC + c0 + 512],
                start=(di == 0), stop=(di == NDT - 1)))
    p.barrier()
    p.op("vector", lambda v: v.tensor_copy(srow2, aux.ps_f32[0:1, 0:2048]))
    p.barrier()
    p.op("vector", lambda v: v.tensor_scalar(srow, srow, 1.0 / D, None, ALU.mult))
    p.op("vector", lambda v: v.tensor_scalar(srow2, srow2, 1.0 / D, None, ALU.mult))
    p.barrier()
    p.op("vector", lambda v: v.tensor_mul(msq, srow, srow))
    p.barrier()
    p.op("vector", lambda v: v.tensor_sub(srow2, srow2, msq))
    p.barrier()
    p.op("vector", lambda v: v.tensor_scalar(srow2, srow2, 1e-5, None, ALU.add))
    p.barrier()
    p.op("scalar", lambda s: s.activation(out=srow2, in_=srow2, func=AF.Ln))
    p.barrier()
    p.op("scalar", lambda s: s.activation(out=srow2, in_=srow2, func=AF.Exp, scale=-0.5))
    p.barrier()
    p.op("vector", lambda v: v.tensor_scalar(srow, srow, -1.0, None, ALU.mult))
    p.barrier()
    for ci, c0 in enumerate(CH4):
        p.op("tensor", lambda t, c0=c0: t.matmul(
            aux.ps_f32[:, c0:c0 + 512], aux.ones_row[0:1, :], srow[0:1, c0:c0 + 512],
            start=True, stop=True))
    p.barrier()
    for di in range(NDT):
        p.op("vector", lambda v, di=di: v.tensor_add(
            xh[:, di * TPC: (di + 1) * TPC],
            xmid[:, di * QLW + fsh(di): di * QLW + fsh(di) + TPC],
            aux.ps_f32[:, 0:2048]))
    p.barrier()
    for ci, c0 in enumerate(CH4):
        p.op("tensor", lambda t, c0=c0: t.matmul(
            aux.ps_f32[:, c0:c0 + 512], aux.ones_row_bf[0:1, :], srow2[0:1, c0:c0 + 512],
            start=True, stop=True))
    p.barrier()
    for di in range(NDT):
        p.op("vector", lambda v, di=di: v.tensor_mul(
            xh[:, di * TPC:(di + 1) * TPC], xh[:, di * TPC:(di + 1) * TPC],
            aux.ps_f32[:, 0:2048]))
    p.barrier()
    for di in range(NDT):
        p.op("vector", lambda v, di=di: v.tensor_scalar(
            xh[:, di * TPC:(di + 1) * TPC], xh[:, di * TPC:(di + 1) * TPC],
            colsM[:, NDT + di:NDT + di + 1], colsM[:, 2 * NDT + di:2 * NDT + di + 1],
            ALU.mult, ALU.add))
    p.barrier()

    for fp in range(22):
        p.dma("gpsimd", wstr[fp % 2][:, 0:NDT * 256].rearrange("p (a c) -> p a c", c=256),
              w1[:, fp * 256: (fp + 1) * 256].rearrange("(a p) c -> p a c", p=128))
        p.barrier()
        for half in range(2):
            for ci, c0 in enumerate(CH4):
                for k in range(NDT):
                    p.op("tensor", lambda t, fp=fp, half=half, c0=c0, k=k: t.matmul(
                        aux.ps_f32[:, c0:c0 + 512],
                        wstr[fp % 2][:, k * 256 + half * 128: k * 256 + half * 128 + 128],
                        xh[:, k * TPC + c0: k * TPC + c0 + 512],
                        start=(k == 0), stop=(k == NDT - 1)))
            p.barrier()
            if half == 0:
                p.op("vector", lambda v, fp=fp: v.tensor_scalar(
                    atmp, aux.ps_f32[:, 0:2048], colsA[:, fp:fp + 1], None, ALU.add))
            else:
                p.op("scalar", lambda s, fp=fp: s.activation(
                    out=gtmp, in_=aux.ps_f32[:, 0:2048], func=AF.Gelu,
                    bias=colsG[:, fp:fp + 1]))
            p.barrier()
        p.op("vector", lambda v, fp=fp: v.tensor_mul(
            u_bf[:, fp * TPC: (fp + 1) * TPC], atmp, gtmp))
        p.barrier()

    for di in range(NDT):
        p.dma("gpsimd", wstr[di % 2][:, 0:22 * 128].rearrange("p (a c) -> p a c", c=128),
              w2[:, di * 128:(di + 1) * 128].rearrange("(a p) c -> p a c", p=128))
        p.barrier()
        for ci, c0 in enumerate(CH4):
            for k in range(22):
                p.op("tensor", lambda t, di=di, c0=c0, k=k: t.matmul(
                    aux.ps_f32[:, c0:c0 + 512],
                    wstr[di % 2][:, k * 128:(k + 1) * 128],
                    u_bf[:, k * TPC + c0: k * TPC + c0 + 512],
                    start=(k == 0), stop=(k == 21)))
        p.barrier()
        for ci, c0 in enumerate(CH4):
            p.dma("sync", resa_t[ci % 2][:, :], res_a[:, di * TPC + c0: di * TPC + c0 + 512])
            p.dma("scalar", resb_t[ci % 2][:, :], res_b[:, di * TPC + c0: di * TPC + c0 + 512])
            p.barrier()
            p.op("vector", lambda v, di=di, c0=c0, ci=ci: v.tensor_add(
                t32[ci % 2][:, :], aux.ps_f32[:, c0:c0 + 512],
                xmid[:, di * QLW + 128 + c0: di * QLW + 128 + c0 + 512]))
            p.op("vector", lambda v, di=di, ci=ci: v.tensor_scalar(
                t32[ci % 2][:, :], t32[ci % 2][:, :], colsM[:, di:di + 1], None, ALU.add))
            p.op("vector", lambda v, ci=ci: v.tensor_add(
                t32[ci % 2][:, :], t32[ci % 2][:, :], resa_t[ci % 2][:, :]))
            p.op("vector", lambda v, ci=ci: v.tensor_add(
                t32[ci % 2][:, :], t32[ci % 2][:, :], resb_t[ci % 2][:, :]))
            p.barrier()
            p.dma("gpsimd", o_out[:, di * TPC + c0: di * TPC + c0 + 512], t32[ci % 2][:, :])
        p.barrier()
    p.finish()
    return p


def _l1f_rope_tables(start):
    half = np.arange(0, 64, 2, dtype=np.float64)
    inv = 1.0 / 10000.0 ** (half / 64)
    t_abs = np.arange(start - 1024, start + 2048).astype(np.float64)
    freqs = t_abs[:, None] * inv[None, :]
    freqs = np.concatenate([freqs, freqs], 1)
    base = (half + 0.4 * 64) / (1.4 * 64)
    power = (t_abs - 8192 // 2) / (512 // 2)
    scale = base[None, :] ** power[:, None]
    scale = np.concatenate([scale, scale], 1)
    cos = np.cos(freqs)
    sin = np.sin(freqs)
    out = np.stack([np.tile((cos * scale).T, (2, 1)), np.tile((sin * scale).T, (2, 1)),
                    np.tile((cos / scale).T, (2, 1)), np.tile((sin / scale).T, (2, 1))])
    return np.ascontiguousarray(out.astype(np.float32))


def _l1f_masks(start):
    import ml_dtypes
    m = np.zeros((NQB, 128, 1024), np.float32)
    for qb in range(NQB):
        aq = start - 128 + qb * 128 + np.arange(128)
        kls = 512 * ((qb - 1) // 4) + 512
        ak = start - 1024 + kls + np.arange(1024)
        bad = (aq[:, None] < ak[None, :]) | (aq[:, None] > ak[None, :] + 512) | (ak[None, :] < 0)
        m[qb][bad] = -1e9
    return m.astype(ml_dtypes.bfloat16)


def host_l1f(x_list, Wl, res_a, res_b, full=False):
    import ml_dtypes
    BFD = ml_dtypes.bfloat16
    p = _get_prog("l1f", build_l1f)
    w1p = np.zeros((D, 5632), np.float32)
    b1a = np.zeros((128, 22), np.float32)
    b1g = np.zeros((128, 22), np.float32)
    w2p = np.zeros((2816, D), np.float32)
    for fp in range(22):
        n = min(128, 2730 - fp * 128)
        w1p[:, fp * 256: fp * 256 + n] = Wl["w1"][:, fp * 128: fp * 128 + n]
        w1p[:, fp * 256 + 128: fp * 256 + 128 + n] = Wl["w1"][:, 2730 + fp * 128: 2730 + fp * 128 + n]
        b1a[0:n, fp] = Wl["b1"][fp * 128: fp * 128 + n]
        b1g[0:n, fp] = Wl["b1"][2730 + fp * 128: 2730 + fp * 128 + n]
        w2p[fp * 128: fp * 128 + n, :] = Wl["w2"][fp * 128: fp * 128 + n, :]
    com = {
        "w_qkv": np.ascontiguousarray(Wl["wqkv"].astype(np.float32)),
        "w_wo": np.ascontiguousarray(Wl["wo"].astype(np.float32)),
        "w_w1": w1p, "w_w2": w2p, "w_b1a": b1a, "w_b1g": b1g,
        "w_b2": np.ascontiguousarray(Wl["b2"].reshape(NDT, 128).T).astype(np.float32),
        "w_lng": np.ascontiguousarray(Wl["lng"].reshape(NDT, 128).T).astype(np.float32),
        "w_lnb": np.ascontiguousarray(Wl["lnb"].reshape(NDT, 128).T).astype(np.float32),
        "w_qs": np.tile(Wl["qs"], 2).reshape(128, 1).astype(np.float32),
        "w_ks": np.tile(Wl["ks"], 2).reshape(128, 1).astype(np.float32),
    }
    zs_a = np.zeros((128, NDT * TPC), np.float32)
    zs_b = np.zeros((128, NDT * TPC), BFD)
    maps = []
    for c in range(NCORE):
        b, s = c // 4, c % 4
        start = s * TPC
        xcur = np.asarray(x_list[c], np.float32)
        prv = np.asarray(x_list[c - 1], np.float32) if s > 0 else None
        cols = []
        for di in range(NDT):
            if prv is not None:
                lo = prv[:, di * TPC + TPC - 1152: (di + 1) * TPC]
            else:
                lo = np.zeros((128, 1152), np.float32)
            cols.append(np.concatenate([lo, xcur[:, di * TPC:(di + 1) * TPC]], axis=1))
        xh = np.ascontiguousarray(np.concatenate(cols, axis=1))
        maps.append({
            "x_l1f": xh, "rope": _l1f_rope_tables(start), "masks": _l1f_masks(start),
            "res_a": res_a[c] if res_a is not None else zs_a,
            "res_b": res_b[c] if res_b is not None else zs_b,
            **com,
        })
    r = p.run(maps)
    if full:
        return r
    return [np.asarray(r[c]["xl_out"]) for c in range(NCORE)]


W2H = 2052




def emit_ln_fm_gb(p, aux, buf, W, n_d, gcol, bcol, srow, srow2, sqbuf, eps=1e-5,
                  stride=None):
    stride = stride or W
    chunks = []
    c0 = 0
    while c0 < W:
        cn = min(512, W - c0)
        chunks.append((c0, cn))
        c0 += cn
    for ci, (c0, cn) in enumerate(chunks):
        for di in range(n_d):
            p.op("tensor", lambda t, di=di, c0=c0, cn=cn, ci=ci: t.matmul(
                aux.ps_f32[0:1, (ci % 2) * 512: (ci % 2) * 512 + cn], aux.ones_col_bf[:, 0:1],
                buf[:, di * stride + c0: di * stride + c0 + cn],
                start=(di == 0), stop=(di == n_d - 1)))
        if ci > 0:
            pc0, pcn = chunks[ci - 1]
            p.op("vector", lambda v, pc0=pc0, pcn=pcn, ci=ci: v.tensor_copy(
                srow[0:1, pc0:pc0 + pcn],
                aux.ps_f32[0:1, ((ci - 1) % 2) * 512: ((ci - 1) % 2) * 512 + pcn]))
        p.barrier()
    pc0, pcn = chunks[-1]
    p.op("vector", lambda v: v.tensor_copy(
        srow[0:1, pc0:pc0 + pcn],
        aux.ps_f32[0:1, ((len(chunks) - 1) % 2) * 512: ((len(chunks) - 1) % 2) * 512 + pcn]))
    p.barrier()
    for ci, (c0, cn) in enumerate(chunks):
        p.op("scalar", lambda s, c0=c0, cn=cn: s.activation(
            out=sqbuf[:, 0:cn], in_=buf[:, c0:c0 + cn], func=AF.Square))
        p.barrier()
        for di in range(n_d):
            sl = (di % 2) * 512
            p.op("tensor", lambda t, di=di, cn=cn, sl=sl: t.matmul(
                aux.ps_f32[0:1, 1024:1024 + cn], aux.ones_col_bf[:, 0:1],
                sqbuf[:, sl:sl + cn], start=(di == 0), stop=(di == n_d - 1)))
            if di + 1 < n_d:
                nsl = ((di + 1) % 2) * 512
                p.op("scalar", lambda s, di=di, c0=c0, cn=cn, nsl=nsl: s.activation(
                    out=sqbuf[:, nsl:nsl + cn],
                    in_=buf[:, (di + 1) * stride + c0: (di + 1) * stride + c0 + cn],
                    func=AF.Square))
            p.barrier()
        p.op("vector", lambda v, c0=c0, cn=cn: v.tensor_copy(
            srow2[0:1, c0:c0 + cn], aux.ps_f32[0:1, 1024:1024 + cn]))
        p.barrier()
    nd128 = n_d * 128
    p.op("vector", lambda v: v.tensor_scalar(srow[0:1, 0:W], srow[0:1, 0:W], 1.0 / nd128, None, ALU.mult))
    p.op("vector", lambda v: v.tensor_scalar(srow2[0:1, 0:W], srow2[0:1, 0:W], 1.0 / nd128, None, ALU.mult))
    p.barrier()
    p.op("vector", lambda v: v.tensor_mul(sqbuf[0:1, 0:W], srow[0:1, 0:W], srow[0:1, 0:W]))
    p.barrier()
    p.op("vector", lambda v: v.tensor_sub(srow2[0:1, 0:W], srow2[0:1, 0:W], sqbuf[0:1, 0:W]))
    p.barrier()
    p.op("vector", lambda v: v.tensor_scalar(srow2[0:1, 0:W], srow2[0:1, 0:W], float(eps), None, ALU.add))
    p.barrier()
    p.op("scalar", lambda s: s.activation(out=srow2[0:1, 0:W], in_=srow2[0:1, 0:W], func=AF.Ln))
    p.barrier()
    p.op("scalar", lambda s: s.activation(out=srow2[0:1, 0:W], in_=srow2[0:1, 0:W], func=AF.Exp, scale=-0.5))
    p.barrier()
    p.op("vector", lambda v: v.tensor_scalar(srow[0:1, 0:W], srow[0:1, 0:W], -1.0, None, ALU.mult))
    p.barrier()
    for ci, (c0, cn) in enumerate(chunks):
        p.op("tensor", lambda t, c0=c0, cn=cn: t.matmul(
            aux.ps_f32[:, 0:cn], aux.ones_row[0:1, :], srow[0:1, c0:c0 + cn],
            start=True, stop=True))
        p.op("tensor", lambda t, c0=c0, cn=cn: t.matmul(
            aux.ps_f32[:, 512:512 + cn], aux.ones_row_bf[0:1, :], srow2[0:1, c0:c0 + cn],
            start=True, stop=True))
        p.barrier()
        for di in range(n_d):
            p.op("vector", lambda v, di=di, c0=c0, cn=cn: v.tensor_add(
                buf[:, di * stride + c0: di * stride + c0 + cn],
                buf[:, di * stride + c0: di * stride + c0 + cn], aux.ps_f32[:, 0:cn]))
        p.barrier()
        for di in range(n_d):
            p.op("vector", lambda v, di=di, c0=c0, cn=cn: v.tensor_mul(
                buf[:, di * stride + c0: di * stride + c0 + cn],
                buf[:, di * stride + c0: di * stride + c0 + cn], aux.ps_f32[:, 512:512 + cn]))
        p.barrier()
        if gcol is not None:
            for di in range(n_d):
                p.op("vector", lambda v, di=di, c0=c0, cn=cn: v.tensor_scalar(
                    buf[:, di * stride + c0: di * stride + c0 + cn],
                    buf[:, di * stride + c0: di * stride + c0 + cn],
                    gcol(di), bcol(di), ALU.mult, ALU.add))
            p.barrier()


def build_l2a():
    p = Prog("l2a")
    xnh = p.input("xn2h", (128, NDT * W2H), BF16)
    bnd = p.input("bnd", (128, 1))
    g_att = p.input("w_gatt", (128, NDT))
    b_att = p.input("w_batt", (128, NDT))
    mixk = p.input("w_mixk", (128, NDT))
    mixv = p.input("w_mixv", (128, NDT))
    mixr = p.input("w_mixr", (128, NDT))
    wk = p.input("w_wk", (D, D))
    wv = p.input("w_wv", (D, D))
    wr = p.input("w_wr", (D, D))
    tdec = p.input("w_tdec", (128, NDT))
    tfir = p.input("w_tfir", (128, NDT))
    wrout = p.input("w_router", (D, 8))
    x2o = p.input("x2own", (128, NDT * TPC))
    lnr = p.input("lnrows", (2, TPC))
    o_P = p.output("P_loc", (128, NDT * TPC))
    o_Q = p.output("Q_loc", (128, NDT * TPC))
    o_ku = p.output("ku", (128, NDT * TPC))
    o_kd = p.output("kd", (128, NDT * TPC))
    o_r = p.output("r_sig", (128, NDT * TPC), BF16)
    o_E = p.output("wkvE", (2, 128, NDT))
    o_cnt = p.output("cnt", (2, 8))
    o_gt = p.output("gates_tm", (TPC, 8))

    aux = Aux(p)
    H = p.sb("l2a_h", [128, NDT * W2H], BF16)
    MIXB = p.sb("l2a_mix", [128, NDT * TPC], BF16)
    kb = p.sb("l2a_kb", [128, TPC], F32)
    ekb = p.sb("l2a_ekb", [128, TPC], F32)
    vb = p.sb("l2a_vb", [128, TPC], F32)
    ab = p.sb("l2a_ab", [128, TPC], F32)
    pb = p.sb("l2a_pb", [128, TPC], F32)
    cols = p.sb("l2a_cols", [128, 12 * NDT], F32)
    lamc = p.sb("l2a_lam", [128, NDT], F32)
    srow = p.sb("l2a_srow", [1, W2H], F32)
    srow2 = p.sb("l2a_srow2", [1, W2H], BF16)
    sqs = p.sb("l2a_sqs", [128, W2H], BF16)
    wst = [p.sb(f"l2a_w{i}", [128, NDT * 128], BF16) for i in range(2)]
    LOB = p.sb("l2a_lob", [128, NDT * TPC], BF16)
    xtmp = p.sb("l2a_xtmp", [128, TPC], F32)
    lrow = p.sb("l2a_lrow", [1, 2 * TPC], F32)
    rows = p.sb("l2a_rows", [128, 8], F32)
    gts = p.sb("l2a_gts", [128, 8], F32)
    m1c = p.sb("l2a_m1c", [128, 8], F32)
    m2c = p.sb("l2a_m2c", [128, 8], F32)
    mxc = p.sb("l2a_mxc", [128, 1], F32)
    ecol = p.sb("l2a_ecol", [128, 2 * NDT], F32)
    s_vd = p.scratch("s_vd", (128, NDT * TPC))

    p.start()
    aux.init_phase()
    p.dma("sync", H[:, :], xnh[:, :])
    for off, ap_ in ((0, g_att), (1, b_att), (2, mixk), (3, mixv), (4, mixr),
                     (5, tdec), (6, tfir)):
        p.dma("scalar", cols[:, off * NDT: (off + 1) * NDT], ap_[:, :])
    p.dma("scalar", cols[:, 7 * NDT: 7 * NDT + 1], bnd[:, :])
    p.barrier()
    for off, src_off in ((8, 2), (9, 3), (10, 4)):
        p.op("vector", lambda v, off=off, src_off=src_off: v.tensor_scalar(
            cols[:, off * NDT: (off + 1) * NDT], cols[:, src_off * NDT: (src_off + 1) * NDT],
            -1.0, 1.0, ALU.mult, ALU.add))
    p.op("scalar", lambda s: s.activation(out=lamc[:, :], in_=cols[:, 5 * NDT: 6 * NDT], func=AF.Exp))
    p.barrier()
    p.op("vector", lambda v: v.tensor_scalar(lamc[:, :], lamc[:, :], -1.0, None, ALU.mult))
    p.barrier()
    p.op("scalar", lambda s: s.activation(out=lamc[:, :], in_=lamc[:, :], func=AF.Exp))
    p.barrier()

    wr8 = p.sb("l2a_wr8", [128, NDT * 8], BF16)
    wrlo = p.sb("l2a_wrlo", [128, NDT * 8], BF16)
    wr32 = p.sb("l2a_wr32", [128, NDT * 8], F32)
    p.dma("sync", wr32[:, :].rearrange("p (a c) -> p a c", c=8),
          wrout.rearrange("(a p) c -> p a c", p=128))
    p.dma("scalar", lrow[0:1, 0:TPC], lnr[0:1, :])
    p.dma("scalar", lrow[0:1, TPC:2 * TPC], lnr[1:2, :])
    p.barrier()
    p.op("vector", lambda v: v.tensor_copy(wr8[:, :], wr32[:, :]))
    p.barrier()
    p.op("vector", lambda v: v.tensor_sub(wr32[:, :], wr32[:, :], wr8[:, :]))
    p.barrier()
    p.op("vector", lambda v: v.tensor_copy(wrlo[:, :], wr32[:, :]))
    p.barrier()
    for dt in range(NDT):
        p.dma("sync", xtmp[:, :], x2o[:, dt * TPC:(dt + 1) * TPC])
        p.barrier()
        for c0 in (0, 512, 1024, 1536):
            p.op("tensor", lambda t, c0=c0: t.matmul(
                aux.ps_f32[:, 0:512], aux.ones_row[0:1, :], lrow[0:1, c0:c0 + 512],
                start=True, stop=True))
            p.op("tensor", lambda t, c0=c0: t.matmul(
                aux.ps_f32[:, 512:1024], aux.ones_row[0:1, :],
                lrow[0:1, TPC + c0: TPC + c0 + 512], start=True, stop=True))
            p.barrier()
            p.op("vector", lambda v, c0=c0: v.tensor_add(
                xtmp[:, c0:c0 + 512], xtmp[:, c0:c0 + 512], aux.ps_f32[:, 0:512]))
            p.barrier()
            p.op("vector", lambda v, c0=c0: v.tensor_mul(
                xtmp[:, c0:c0 + 512], xtmp[:, c0:c0 + 512], aux.ps_f32[:, 512:1024]))
            p.barrier()
        p.op("vector", lambda v, dt=dt: v.tensor_copy(
            H[:, dt * W2H + 4: dt * W2H + 4 + TPC], xtmp[:, :]))
        p.barrier()
        p.op("vector", lambda v, dt=dt: v.tensor_sub(
            xtmp[:, :], xtmp[:, :], H[:, dt * W2H + 4: dt * W2H + 4 + TPC]))
        p.barrier()
        p.op("vector", lambda v, dt=dt: v.tensor_copy(LOB[:, dt * TPC:(dt + 1) * TPC], xtmp[:, :]))
        p.barrier()
    for tt in range(16):
        for hi_lhs, rhs, fi in ((True, wr8, 0), (False, wr8, 1), (True, wrlo, 2)):
            for k in range(NDT):
                p.op("tensor", lambda t, tt=tt, k=k, hi_lhs=hi_lhs, rhs=rhs, fi=fi: t.matmul(
                    aux.ps_f32[:, 0:8],
                    (H[:, k * W2H + 4 + tt * 128: k * W2H + 4 + (tt + 1) * 128] if hi_lhs
                     else LOB[:, k * TPC + tt * 128: k * TPC + (tt + 1) * 128]),
                    rhs[:, k * 8: (k + 1) * 8],
                    start=(fi == 0 and k == 0), stop=(fi == 2 and k == NDT - 1)))
        p.barrier()
        p.op("vector", lambda v: v.reduce_max(mxc[:, :], aux.ps_f32[:, 0:8], axis=AX.X, negate=True))
        p.barrier()
        p.op("scalar", lambda s: s.activation(out=gts[:, :], in_=aux.ps_f32[:, 0:8], func=AF.Exp,
                                              bias=mxc[:, 0:1], accum_out=mxc[:, 0:1]))
        p.barrier()
        p.op("vector", lambda v: v.reciprocal(mxc[:, :], mxc[:, :]))
        p.barrier()
        p.op("vector", lambda v: v.tensor_scalar(gts[:, :], gts[:, :], mxc[:, 0:1], None, ALU.mult))
        p.barrier()
        p.dma("gpsimd", o_gt[tt * 128:(tt + 1) * 128, :], gts[:, :])
        p.op("vector", lambda v: v.reduce_max(mxc[:, :], gts[:, :], axis=AX.X))
        p.barrier()
        p.op("vector", lambda v: v.tensor_scalar(m1c[:, :], gts[:, :], mxc[:, 0:1], None, ALU.is_ge))
        p.barrier()
        p.op("vector", lambda v: v.tensor_scalar(rows[:, :], m1c[:, :], -1.0, 1.0, ALU.mult, ALU.add))
        p.barrier()
        p.op("vector", lambda v: v.tensor_mul(rows[:, :], rows[:, :], gts[:, :]))
        p.barrier()
        p.op("vector", lambda v: v.reduce_max(mxc[:, :], rows[:, :], axis=AX.X))
        p.barrier()
        p.op("vector", lambda v: v.tensor_scalar(m2c[:, :], rows[:, :], mxc[:, 0:1], None, ALU.is_ge))
        p.barrier()
        p.op("tensor", lambda t, tt=tt: t.matmul(
            aux.ps_f32[0:1, 512:520], aux.ones_col[:, 0:1], m1c[:, :],
            start=(tt == 0), stop=(tt == 15)))
        p.op("tensor", lambda t, tt=tt: t.matmul(
            aux.ps_f32[0:1, 1024:1032], aux.ones_col[:, 0:1], m2c[:, :],
            start=(tt == 0), stop=(tt == 15)))
        p.barrier()
    p.op("vector", lambda v: v.tensor_copy(rows[0:1, 0:8], aux.ps_f32[0:1, 512:520]))
    p.barrier()
    p.dma("scalar", o_cnt[0:1, :], rows[0:1, 0:8])
    p.barrier()
    p.op("vector", lambda v: v.tensor_copy(rows[0:1, 0:8], aux.ps_f32[0:1, 1024:1032]))
    p.barrier()
    p.dma("scalar", o_cnt[1:2, :], rows[0:1, 0:8])
    p.barrier()

    emit_ln_fm_gb(p, aux, H, W2H, NDT,
                  lambda di: cols[:, 0 * NDT + di: 0 * NDT + di + 1],
                  lambda di: cols[:, 1 * NDT + di: 1 * NDT + di + 1],
                  srow, srow2, sqs)
    for di in range(NDT):
        p.op("vector", lambda v, di=di: v.tensor_scalar(
            H[:, di * W2H: di * W2H + 4], H[:, di * W2H: di * W2H + 4],
            cols[:, 7 * NDT: 7 * NDT + 1], None, ALU.mult))
    p.barrier()

    def emit_mix(mix_off, onemix_off):
        for di in range(NDT):
            p.op("vector", lambda v, di=di: v.tensor_scalar(
                MIXB[:, di * TPC:(di + 1) * TPC],
                H[:, di * W2H + 3: di * W2H + 3 + TPC],
                cols[:, onemix_off * NDT + di: onemix_off * NDT + di + 1], None, ALU.mult))
        p.barrier()
        for di in range(NDT):
            p.op("vector", lambda v, di=di, mix_off=mix_off: v.scalar_tensor_tensor(
                MIXB[:, di * TPC:(di + 1) * TPC],
                H[:, di * W2H + 4: di * W2H + 4 + TPC],
                cols[:, mix_off * NDT + di: mix_off * NDT + di + 1],
                MIXB[:, di * TPC:(di + 1) * TPC], ALU.mult, ALU.add))
        p.barrier()

    def stream_w(win, parity, di):
        p.dma("gpsimd", wst[parity][:, :].rearrange("p (a c) -> p a c", c=128),
              win[:, di * 128:(di + 1) * 128].rearrange("(a p) c -> p a c", p=128))

    emit_mix(3, 9)
    stream_w(wv, 0, 0)
    p.barrier()
    for di in range(NDT):
        if di + 1 < NDT:
            stream_w(wv, (di + 1) % 2, di + 1)
        for c0 in (0, 512, 1024, 1536):
            for k in range(NDT):
                p.op("tensor", lambda t, di=di, c0=c0, k=k: t.matmul(
                    aux.ps_f32[:, c0:c0 + 512],
                    wst[di % 2][:, k * 128:(k + 1) * 128],
                    MIXB[:, k * TPC + c0: k * TPC + c0 + 512],
                    start=(k == 0), stop=(k == NDT - 1)))
        p.barrier()
        p.op("vector", lambda v: v.tensor_copy(vb[:, :], aux.ps_f32[:, 0:2048]))
        p.barrier()
        p.dma("gpsimd", s_vd[:, di * TPC:(di + 1) * TPC], vb[:, :])
        p.barrier()

    emit_mix(2, 8)
    stream_w(wk, 0, 0)
    p.barrier()
    for di in range(NDT):
        if di + 1 < NDT:
            stream_w(wk, (di + 1) % 2, di + 1)
        p.dma("sync", vb[:, :], s_vd[:, di * TPC:(di + 1) * TPC])
        for c0 in (0, 512, 1024, 1536):
            for k in range(NDT):
                p.op("tensor", lambda t, di=di, c0=c0, k=k: t.matmul(
                    aux.ps_f32[:, c0:c0 + 512],
                    wst[di % 2][:, k * 128:(k + 1) * 128],
                    MIXB[:, k * TPC + c0: k * TPC + c0 + 512],
                    start=(k == 0), stop=(k == NDT - 1)))
        p.barrier()
        p.op("vector", lambda v: v.tensor_copy(kb[:, :], aux.ps_f32[:, 0:2048]))
        p.barrier()
        p.op("scalar", lambda s: s.activation(out=ekb[:, :], in_=kb[:, :], func=AF.Exp))
        p.barrier()
        p.op("scalar", lambda s, di=di: s.activation(
            out=kb[:, :], in_=kb[:, :], func=AF.Exp, bias=cols[:, 6 * NDT + di: 6 * NDT + di + 1]))
        p.barrier()
        p.dma("gpsimd", o_kd[:, di * TPC:(di + 1) * TPC], kb[:, :])
        p.op("vector", lambda v: v.tensor_mul(ab[:, :], kb[:, :], vb[:, :]))
        p.barrier()
        p.dma("gpsimd", o_ku[:, di * TPC:(di + 1) * TPC], ab[:, :])
        p.barrier()
        p.op("vector", lambda v: v.tensor_mul(ab[:, :], ekb[:, :], vb[:, :]))
        p.barrier()
        p.op("vector", lambda v, di=di: v.tensor_tensor_scan(
            pb[:, :], lamc[:, di:di + 1].broadcast_to([128, TPC]), ab[:, :], 0.0,
            ALU.mult, ALU.add))
        p.barrier()
        p.dma("gpsimd", o_P[:, di * TPC:(di + 1) * TPC], pb[:, :])
        p.op("vector", lambda v, di=di: v.tensor_copy(ecol[:, di:di + 1], pb[:, TPC - 1:TPC]))
        p.barrier()
        p.op("vector", lambda v, di=di: v.tensor_tensor_scan(
            pb[:, :], lamc[:, di:di + 1].broadcast_to([128, TPC]), ekb[:, :], 0.0,
            ALU.mult, ALU.add))
        p.barrier()
        p.dma("gpsimd", o_Q[:, di * TPC:(di + 1) * TPC], pb[:, :])
        p.op("vector", lambda v, di=di: v.tensor_copy(ecol[:, NDT + di: NDT + di + 1], pb[:, TPC - 1:TPC]))
        p.barrier()
    p.dma("scalar", o_E[0, :, :], ecol[:, 0:NDT])
    p.dma("scalar", o_E[1, :, :], ecol[:, NDT:2 * NDT])
    p.barrier()

    emit_mix(4, 10)
    stream_w(wr, 0, 0)
    p.barrier()
    for di in range(NDT):
        if di + 1 < NDT:
            stream_w(wr, (di + 1) % 2, di + 1)
        for c0 in (0, 512, 1024, 1536):
            for k in range(NDT):
                p.op("tensor", lambda t, di=di, c0=c0, k=k: t.matmul(
                    aux.ps_f32[:, c0:c0 + 512],
                    wst[di % 2][:, k * 128:(k + 1) * 128],
                    MIXB[:, k * TPC + c0: k * TPC + c0 + 512],
                    start=(k == 0), stop=(k == NDT - 1)))
        p.barrier()
        p.op("scalar", lambda s: s.activation(out=vb[:, :], in_=aux.ps_f32[:, 0:2048], func=AF.Sigmoid))
        p.barrier()
        p.op("vector", lambda v: v.tensor_copy(sqs[:, 0:TPC], vb[:, :]))
        p.barrier()
        p.dma("gpsimd", o_r[:, di * TPC:(di + 1) * TPC], sqs[:, 0:TPC])
        p.barrier()
    p.finish()
    return p


def build_l2b():
    p = Prog("l2b")
    i_P = p.input("P_loc", (128, NDT * TPC))
    i_Q = p.input("Q_loc", (128, NDT * TPC))
    i_ku = p.input("ku", (128, NDT * TPC))
    i_kd = p.input("kd", (128, NDT * TPC))
    i_r = p.input("r_sig", (128, NDT * TPC), BF16)
    i_xn = p.input("xn2", (128, NDT * TPC), BF16)
    i_E = p.input("Eprev", (3, 2, 128, NDT))
    tdec = p.input("w_tdec", (128, NDT))
    tdrow = p.input("w_tdrow", (1, D))
    wo = p.input("w_wo", (D, D))
    o_xmid = p.output("xmid", (128, NDT * TPC))

    aux = Aux(p)
    iota_d = p.const("c_iota", np.arange(1, TPC + 1, dtype=np.float32).reshape(1, TPC))
    iota = p.sb("l2b_iota", [1, TPC], F32)
    RS = p.sb("l2b_rs", [128, NDT * TPC], BF16)
    XN = p.sb("l2b_xn", [128, NDT * TPC], BF16)
    RW = p.sb("l2b_rw", [128, NDT * TPC], BF16)
    Dc = p.sb("l2b_dc", [128, TPC], F32)
    PS = p.sb("l2b_ps", [128, TPC + 1], F32)
    QS = p.sb("l2b_qs", [128, TPC + 1], F32)
    Pb = p.sb("l2b_pb", [128, TPC], F32)
    Qb = p.sb("l2b_qb", [128, TPC], F32)
    kub = p.sb("l2b_kub", [128, TPC], F32)
    kdb = p.sb("l2b_kdb", [128, TPC], F32)
    OB = p.sb("l2b_ob", [128, TPC], F32)
    wst = [p.sb(f"l2b_w{i}", [128, NDT * 128], BF16) for i in range(2)]
    wrow = p.sb("l2b_wrow", [1, D], F32)
    lamw = p.sb("l2b_lamw", [128, NDT], F32)
    lam2k = p.sb("l2b_lam2k", [128, NDT], F32)
    ecol = p.sb("l2b_ecol", [128, 6 * NDT], F32)
    pinc = p.sb("l2b_pin", [128, 2 * NDT], F32)

    p.start()
    aux.init_phase()
    p.dma("sync", RS[:, :], i_r[:, :])
    p.dma("sync", XN[:, :], i_xn[:, :])
    p.dma("scalar", lamw[:, :], tdec[:, :])
    p.dma("scalar", wrow[:, :], tdrow[:, :])
    p.dma("scalar", iota[:, :], iota_d[:, :])
    for j in range(3):
        p.dma("scalar", ecol[:, j * NDT:(j + 1) * NDT], i_E[j, 0, :, :])
        p.dma("scalar", ecol[:, (3 + j) * NDT:(4 + j) * NDT], i_E[j, 1, :, :])
    p.barrier()
    p.op("scalar", lambda s: s.activation(out=lamw[:, :], in_=lamw[:, :], func=AF.Exp))
    p.op("scalar", lambda s: s.activation(out=wrow[:, :], in_=wrow[:, :], func=AF.Exp))
    p.barrier()
    p.op("vector", lambda v: v.tensor_scalar(lamw[:, :], lamw[:, :], -1.0, None, ALU.mult))
    p.op("vector", lambda v: v.tensor_scalar(wrow[:, :], wrow[:, :], -1.0, None, ALU.mult))
    p.barrier()
    p.op("scalar", lambda s: s.activation(out=lam2k[:, :], in_=lamw[:, :], func=AF.Exp,
                                          scale=float(TPC)))
    p.barrier()
    for sl in range(2):
        b0 = 3 * sl * NDT
        p.op("vector", lambda v, sl=sl, b0=b0: v.tensor_mul(
            pinc[:, sl * NDT:(sl + 1) * NDT], lam2k[:, :], ecol[:, b0 + 2 * NDT: b0 + 3 * NDT]))
        p.barrier()
        p.op("vector", lambda v, sl=sl, b0=b0: v.tensor_add(
            pinc[:, sl * NDT:(sl + 1) * NDT], pinc[:, sl * NDT:(sl + 1) * NDT],
            ecol[:, b0 + NDT: b0 + 2 * NDT]))
        p.barrier()
        p.op("vector", lambda v, sl=sl: v.tensor_mul(
            pinc[:, sl * NDT:(sl + 1) * NDT], pinc[:, sl * NDT:(sl + 1) * NDT], lam2k[:, :]))
        p.barrier()
        p.op("vector", lambda v, sl=sl, b0=b0: v.tensor_add(
            pinc[:, sl * NDT:(sl + 1) * NDT], pinc[:, sl * NDT:(sl + 1) * NDT],
            ecol[:, b0: b0 + NDT]))
        p.barrier()

    for dt in range(NDT):
        p.dma("sync", Pb[:, :], i_P[:, dt * TPC:(dt + 1) * TPC])
        p.dma("sync", Qb[:, :], i_Q[:, dt * TPC:(dt + 1) * TPC])
        p.dma("sync", kub[:, :], i_ku[:, dt * TPC:(dt + 1) * TPC])
        p.dma("sync", kdb[:, :], i_kd[:, dt * TPC:(dt + 1) * TPC])
        for c0 in (0, 512, 1024, 1536):
            p.op("tensor", lambda t, dt=dt, c0=c0: t.matmul(
                aux.ps_f32[:, c0:c0 + 512], wrow[0:1, dt * 128:(dt + 1) * 128],
                iota[0:1, c0:c0 + 512], start=True, stop=True))
        p.barrier()
        p.op("scalar", lambda s: s.activation(out=Dc[:, :], in_=aux.ps_f32[:, 0:TPC], func=AF.Exp))
        p.barrier()
        p.op("vector", lambda v, dt=dt: v.scalar_tensor_tensor(
            PS[:, 1:TPC + 1], Dc[:, :], pinc[:, dt:dt + 1], Pb[:, :], ALU.mult, ALU.add))
        p.op("vector", lambda v, dt=dt: v.tensor_copy(PS[:, 0:1], pinc[:, dt:dt + 1]))
        p.op("vector", lambda v, dt=dt: v.scalar_tensor_tensor(
            QS[:, 1:TPC + 1], Dc[:, :], pinc[:, NDT + dt: NDT + dt + 1], Qb[:, :], ALU.mult, ALU.add))
        p.op("vector", lambda v, dt=dt: v.tensor_copy(QS[:, 0:1], pinc[:, NDT + dt: NDT + dt + 1]))
        p.barrier()
        p.op("vector", lambda v: v.tensor_add(kub[:, :], kub[:, :], PS[:, 0:TPC]))
        p.op("vector", lambda v: v.tensor_add(kdb[:, :], kdb[:, :], QS[:, 0:TPC]))
        p.barrier()
        p.op("vector", lambda v: v.reciprocal(kdb[:, :], kdb[:, :]))
        p.barrier()
        p.op("vector", lambda v: v.tensor_mul(kub[:, :], kub[:, :], kdb[:, :]))
        p.barrier()
        p.op("vector", lambda v, dt=dt: v.tensor_mul(
            RW[:, dt * TPC:(dt + 1) * TPC], kub[:, :], RS[:, dt * TPC:(dt + 1) * TPC]))
        p.barrier()

    def stream_w(parity, di):
        p.dma("gpsimd", wst[parity][:, :].rearrange("p (a c) -> p a c", c=128),
              wo[:, di * 128:(di + 1) * 128].rearrange("(a p) c -> p a c", p=128))

    stream_w(0, 0)
    p.barrier()
    for dto in range(NDT):
        if dto + 1 < NDT:
            stream_w((dto + 1) % 2, dto + 1)
        for c0 in (0, 512, 1024, 1536):
            for k in range(NDT):
                p.op("tensor", lambda t, dto=dto, c0=c0, k=k: t.matmul(
                    aux.ps_f32[:, c0:c0 + 512],
                    wst[dto % 2][:, k * 128:(k + 1) * 128],
                    RW[:, k * TPC + c0: k * TPC + c0 + 512],
                    start=(k == 0), stop=(k == NDT - 1)))
        p.barrier()
        for c0 in (0, 512, 1024, 1536):
            p.op("vector", lambda v, dto=dto, c0=c0: v.tensor_add(
                OB[:, c0:c0 + 512], aux.ps_f32[:, c0:c0 + 512],
                XN[:, dto * TPC + c0: dto * TPC + c0 + 512]))
        p.barrier()
        p.dma("gpsimd", o_xmid[:, dto * TPC:(dto + 1) * TPC], OB[:, :])
        p.barrier()
    p.finish()
    return p


def build_l2c():
    p = Prog("l2c")
    xmh = p.input("xmidh", (128, NDT * W2H))
    bnd = p.input("bnd", (128, 1))
    g_f = p.input("w_gffn", (128, NDT))
    b_f = p.input("w_bffn", (128, NDT))
    mixk = p.input("w_mixk2", (128, NDT))
    mixr = p.input("w_mixr2", (128, NDT))
    wkf = p.input("w_wkf", (D, 4 * D))
    wvf = p.input("w_wvf", (4 * D, D))
    wrf = p.input("w_wrf", (D, D))
    i_x2 = p.input("x2", (128, NDT * TPC))
    o_xa = p.output("xa", (128, NDT * TPC))

    aux = Aux(p)
    H = p.sb("l2c_h", [128, NDT * W2H], BF16)
    MIXB = p.sb("l2c_mix", [128, NDT * TPC], BF16)
    RF = p.sb("l2c_rf", [128, NDT * TPC], BF16)
    KFC = p.sb("l2c_kfc", [128, 32 * 512], BF16)
    wkst = [p.sb(f"l2c_wk{i}", [128, NDT * 128], BF16) for i in range(2)]
    wvst = [p.sb(f"l2c_wv{i}", [128, 32 * 128], BF16) for i in range(2)]
    cols = p.sb("l2c_cols", [128, 7 * NDT], F32)
    srow = p.sb("l2c_srow", [1, W2H], F32)
    srow2 = p.sb("l2c_srow2", [1, W2H], BF16)
    sqs = p.sb("l2c_sqs", [128, W2H], BF16)
    OB = p.sb("l2c_ob", [128, 512], F32)
    X2B = p.sb("l2c_x2b", [128, 512], F32)
    XMB = p.sb("l2c_xmb", [128, 512], F32)

    p.start()
    aux.init_phase()
    p.dma("gpsimd", H[:, :], xmh[:, :])
    for off, ap_ in ((0, g_f), (1, b_f), (2, mixk), (3, mixr)):
        p.dma("scalar", cols[:, off * NDT: (off + 1) * NDT], ap_[:, :])
    p.dma("scalar", cols[:, 6 * NDT: 6 * NDT + 1], bnd[:, :])
    p.barrier()
    for off, src_off in ((4, 2), (5, 3)):
        p.op("vector", lambda v, off=off, src_off=src_off: v.tensor_scalar(
            cols[:, off * NDT: (off + 1) * NDT], cols[:, src_off * NDT: (src_off + 1) * NDT],
            -1.0, 1.0, ALU.mult, ALU.add))
    p.barrier()
    emit_ln_fm_gb(p, aux, H, W2H, NDT,
                  lambda di: cols[:, 0 * NDT + di: 0 * NDT + di + 1],
                  lambda di: cols[:, 1 * NDT + di: 1 * NDT + di + 1],
                  srow, srow2, sqs)
    for di in range(NDT):
        p.op("vector", lambda v, di=di: v.tensor_scalar(
            H[:, di * W2H: di * W2H + 4], H[:, di * W2H: di * W2H + 4],
            cols[:, 6 * NDT: 6 * NDT + 1], None, ALU.mult))
    p.barrier()

    def emit_mix(mix_off, onemix_off):
        for di in range(NDT):
            p.op("vector", lambda v, di=di, onemix_off=onemix_off: v.tensor_scalar(
                MIXB[:, di * TPC:(di + 1) * TPC],
                H[:, di * W2H + 3: di * W2H + 3 + TPC],
                cols[:, onemix_off * NDT + di: onemix_off * NDT + di + 1], None, ALU.mult))
        p.barrier()
        for di in range(NDT):
            p.op("vector", lambda v, di=di, mix_off=mix_off: v.scalar_tensor_tensor(
                MIXB[:, di * TPC:(di + 1) * TPC],
                H[:, di * W2H + 4: di * W2H + 4 + TPC],
                cols[:, mix_off * NDT + di: mix_off * NDT + di + 1],
                MIXB[:, di * TPC:(di + 1) * TPC], ALU.mult, ALU.add))
        p.barrier()

    emit_mix(3, 5)
    p.dma("gpsimd", wkst[0][:, :].rearrange("p (a c) -> p a c", c=128),
          wrf[:, 0:128].rearrange("(a p) c -> p a c", p=128))
    p.barrier()
    for di in range(NDT):
        if di + 1 < NDT:
            p.dma("gpsimd", wkst[(di + 1) % 2][:, :].rearrange("p (a c) -> p a c", c=128),
                  wrf[:, (di + 1) * 128:(di + 2) * 128].rearrange("(a p) c -> p a c", p=128))
        for c0 in (0, 512, 1024, 1536):
            for k in range(NDT):
                p.op("tensor", lambda t, di=di, c0=c0, k=k: t.matmul(
                    aux.ps_f32[:, c0:c0 + 512],
                    wkst[di % 2][:, k * 128:(k + 1) * 128],
                    MIXB[:, k * TPC + c0: k * TPC + c0 + 512],
                    start=(k == 0), stop=(k == NDT - 1)))
        p.barrier()
        p.op("scalar", lambda s, di=di: s.activation(
            out=RF[:, di * TPC:(di + 1) * TPC], in_=aux.ps_f32[:, 0:TPC], func=AF.Sigmoid))
        p.barrier()

    emit_mix(2, 4)
    for ch in range(4):
        c0 = ch * 512
        if ch == 0:
            p.dma("gpsimd", wkst[0][:, :].rearrange("p (a c) -> p a c", c=128),
                  wkf[:, 0:128].rearrange("(a p) c -> p a c", p=128))
            p.barrier()
        for ft in range(32):
            if ft + 1 < 32 or ch < 3:
                nft = (ft + 1) % 32
                p.dma("gpsimd", wkst[(ft + 1) % 2][:, :].rearrange("p (a c) -> p a c", c=128),
                      wkf[:, nft * 128:(nft + 1) * 128].rearrange("(a p) c -> p a c", p=128))
            pr = (ft % 2) * 512
            for k in range(NDT):
                p.op("tensor", lambda t, ft=ft, c0=c0, k=k, pr=pr: t.matmul(
                    aux.ps_f32[:, pr:pr + 512],
                    wkst[ft % 2][:, k * 128:(k + 1) * 128],
                    MIXB[:, k * TPC + c0: k * TPC + c0 + 512],
                    start=(k == 0), stop=(k == NDT - 1)))
            p.barrier()
            p.op("vector", lambda v, pr=pr: v.tensor_scalar(
                aux.ps_f32[:, pr:pr + 512], aux.ps_f32[:, pr:pr + 512], 0.0, None, ALU.max))
            p.barrier()
            p.op("scalar", lambda s, ft=ft, pr=pr: s.activation(
                out=KFC[:, ft * 512:(ft + 1) * 512], in_=aux.ps_f32[:, pr:pr + 512],
                func=AF.Square))
            p.barrier()
        p.dma("gpsimd", wvst[0][:, :].rearrange("p (a c) -> p a c", c=128),
              wvf[:, 0:128].rearrange("(a p) c -> p a c", p=128))
        p.barrier()
        for dto in range(NDT):
            if dto + 1 < NDT:
                p.dma("gpsimd", wvst[(dto + 1) % 2][:, :].rearrange("p (a c) -> p a c", c=128),
                      wvf[:, (dto + 1) * 128:(dto + 2) * 128].rearrange("(a p) c -> p a c", p=128))
            p.dma("sync", X2B[:, :], i_x2[:, dto * TPC + c0: dto * TPC + c0 + 512])
            p.dma("sync", XMB[:, :], xmh[:, dto * W2H + 4 + c0: dto * W2H + 4 + c0 + 512])
            for ft in range(32):
                p.op("tensor", lambda t, dto=dto, ft=ft: t.matmul(
                    aux.ps_f32[:, 1024:1536],
                    wvst[dto % 2][:, ft * 128:(ft + 1) * 128],
                    KFC[:, ft * 512:(ft + 1) * 512],
                    start=(ft == 0), stop=(ft == 31)))
            p.barrier()
            p.op("vector", lambda v, dto=dto, c0=c0: v.tensor_mul(
                OB[:, :], aux.ps_f32[:, 1024:1536], RF[:, dto * TPC + c0: dto * TPC + c0 + 512]))
            p.barrier()
            p.op("vector", lambda v: v.tensor_add(OB[:, :], OB[:, :], X2B[:, :]))
            p.barrier()
            p.op("vector", lambda v: v.tensor_add(OB[:, :], OB[:, :], XMB[:, :]))
            p.barrier()
            p.dma("gpsimd", o_xa[:, dto * TPC + c0: dto * TPC + c0 + 512], OB[:, :])
            p.barrier()
    p.finish()
    return p


CAP2 = 5120.0
NTOK_G = 16384


def build_l2d1():
    p = Prog("l2d1")
    xall = p.input("xall", (128, NDT * NTOK_G), BF16)
    w1e = p.input("w1e", (D, 4 * D))
    b1c = p.input("b1c", (128, 32))
    w2e = p.input("w2e", (4 * D, D))
    b2c = p.input("b2c", (128, NDT))
    o_F = p.output("F", (128, NDT * NTOK_G), BF16)

    aux = Aux(p, psum_mode="f32")
    TP = 512
    NTP = NTOK_G // TP
    w1b = p.sb("d1_w1", [128, NDT * 4 * D], BF16)
    w2b = p.sb("d1_w2", [128, 32 * D], BF16)
    HM = p.sb("d1_hm", [128, 32 * TP], BF16)
    XC = [p.sb(f"d1_xc{i}", [128, NDT * TP], BF16) for i in range(2)]
    OBf = p.sb("d1_ob", [128, 2 * TP], BF16)
    bc1 = p.sb("d1_b1c", [128, 32], F32)
    bc2 = p.sb("d1_b2c", [128, NDT], F32)

    p.start()
    aux.init_phase()
    p.dma("scalar", bc1[:, :], b1c[:, :])
    p.dma("scalar", bc2[:, :], b2c[:, :])
    p.dma("gpsimd", w1b[:, :].rearrange("p (a c) -> p a c", c=4 * D),
          w1e.rearrange("(a p) c -> p a c", p=128))
    p.dma("gpsimd", w2b[:, :].rearrange("p (a c) -> p a c", c=D),
          w2e.rearrange("(a p) c -> p a c", p=128))
    for k in range(NDT):
        p.dma("sync", XC[0][:, k * TP:(k + 1) * TP], xall[:, k * NTOK_G: k * NTOK_G + TP])
    p.barrier()
    for tp in range(NTP):
        cur = XC[tp % 2]
        if tp + 1 < NTP:
            nxt = XC[(tp + 1) % 2]
            for k in range(NDT):
                p.dma("sync", nxt[:, k * TP:(k + 1) * TP],
                      xall[:, k * NTOK_G + (tp + 1) * TP: k * NTOK_G + (tp + 2) * TP])
        for ft in range(33):
            if ft < 32:
                pr = (ft % 2) * 512
                for k in range(NDT):
                    p.op("tensor", lambda t, ft=ft, k=k, pr=pr, cur=cur: t.matmul(
                        aux.ps_f32[:, pr:pr + 512],
                        w1b[:, k * 4 * D + ft * 128: k * 4 * D + (ft + 1) * 128],
                        cur[:, k * TP:(k + 1) * TP],
                        start=(k == 0), stop=(k == NDT - 1)))
            if ft > 0:
                fp = ft - 1
                pr = (fp % 2) * 512
                p.op("scalar", lambda s, fp=fp, pr=pr: s.activation(
                    out=HM[:, fp * TP:(fp + 1) * TP], in_=aux.ps_f32[:, pr:pr + 512],
                    func=AF.Silu, bias=bc1[:, fp:fp + 1]))
            p.barrier()
        for dto in range(NDT + 1):
            if dto < NDT:
                pr = 1024 + (dto % 2) * 512
                for ft in range(32):
                    p.op("tensor", lambda t, dto=dto, ft=ft, pr=pr: t.matmul(
                        aux.ps_f32[:, pr:pr + 512],
                        w2b[:, ft * D + dto * 128: ft * D + (dto + 1) * 128],
                        HM[:, ft * TP:(ft + 1) * TP],
                        start=(ft == 0), stop=(ft == 31)))
            if dto > 0:
                dp = dto - 1
                pr = 1024 + (dp % 2) * 512
                p.op("vector", lambda v, dp=dp, pr=pr: v.tensor_scalar(
                    OBf[:, (dp % 2) * TP:(dp % 2) * TP + TP], aux.ps_f32[:, pr:pr + 512],
                    bc2[:, dp:dp + 1], None, ALU.add))
            p.barrier()
            if dto > 0:
                dp = dto - 1
                p.dma("gpsimd", o_F[:, dp * NTOK_G + tp * TP: dp * NTOK_G + (tp + 1) * TP],
                      OBf[:, (dp % 2) * TP:(dp % 2) * TP + TP])
    p.barrier()
    p.finish()
    return p


def build_l2d0():
    p = Prog("l2d0")
    gt_in = p.input("gates_tm", (TPC, 8))
    cp1 = p.input("cntp1T", (8, 8))
    cp2 = p.input("cntp2T", (8, 8))
    ca1 = p.input("cnt1aT", (8, 8))
    o_WT = p.output("WTs", (8, TPC))

    aux = Aux(p, psum_mode="f32")
    GT = p.sb("d0_gt", [128, 8], F32)
    m1c = p.sb("d0_m1c", [128, 8], F32)
    m2c = p.sb("d0_m2c", [128, 8], F32)
    gm1 = p.sb("d0_gm1", [128, 8], F32)
    gm2 = p.sb("d0_gm2", [128, 8], F32)
    mxc = p.sb("d0_mxc", [128, 1], F32)
    M1T = p.sb("d0_m1t", [8, TPC], F32)
    M2T = p.sb("d0_m2t", [8, TPC], F32)
    G1T = p.sb("d0_g1t", [8, TPC], F32)
    G2T = p.sb("d0_g2t", [8, TPC], F32)
    CT = p.sb("d0_ct", [8, TPC], F32)
    pri = p.sb("d0_pri", [8, 8], F32)
    pcol = p.sb("d0_pcol", [8, 3], F32)
    wrow = p.sb("d0_wrow", [1, TPC], F32)

    p.start()
    aux.init_phase()
    for tt in range(16):
        p.dma("sync", GT[:, :], gt_in[tt * 128:(tt + 1) * 128, :])
        p.barrier()
        p.op("vector", lambda v: v.reduce_max(mxc[:, :], GT[:, :], axis=AX.X))
        p.barrier()
        p.op("vector", lambda v: v.tensor_scalar(m1c[:, :], GT[:, :], mxc[:, 0:1], None, ALU.is_ge))
        p.barrier()
        p.op("vector", lambda v: v.tensor_scalar(gm2[:, :], m1c[:, :], -1.0, 1.0, ALU.mult, ALU.add))
        p.barrier()
        p.op("vector", lambda v: v.tensor_mul(gm2[:, :], gm2[:, :], GT[:, :]))
        p.barrier()
        p.op("vector", lambda v: v.reduce_max(mxc[:, :], gm2[:, :], axis=AX.X))
        p.barrier()
        p.op("vector", lambda v: v.tensor_scalar(m2c[:, :], gm2[:, :], mxc[:, 0:1], None, ALU.is_ge))
        p.barrier()
        p.op("vector", lambda v: v.tensor_mul(gm1[:, :], GT[:, :], m1c[:, :]))
        p.op("vector", lambda v: v.tensor_mul(gm2[:, :], GT[:, :], m2c[:, :]))
        p.barrier()
        for j, (src, dst) in enumerate(((m1c, M1T), (m2c, M2T), (gm1, G1T), (gm2, G2T))):
            p.op("tensor", lambda t, src=src, j=j: t.transpose(
                aux.ps_f32[0:8, j * 128:(j + 1) * 128], src[:, :], aux.ident_f32[:, :]))
        p.barrier()
        for j, (src, dst) in enumerate(((m1c, M1T), (m2c, M2T), (gm1, G1T), (gm2, G2T))):
            p.op("vector", lambda v, dst=dst, j=j, tt=tt: v.tensor_copy(
                dst[:, tt * 128:(tt + 1) * 128], aux.ps_f32[0:8, j * 128:(j + 1) * 128]))
        p.barrier()
    p.dma("scalar", pri[:, :], cp1[:, :])
    p.barrier()
    p.op("vector", lambda v: v.reduce_sum(pcol[:, 0:1], pri[:, :], axis=AX.X))
    p.barrier()
    p.dma("scalar", pri[:, :], cp2[:, :])
    p.barrier()
    p.op("vector", lambda v: v.reduce_sum(pcol[:, 1:2], pri[:, :], axis=AX.X))
    p.barrier()
    p.dma("scalar", pri[:, :], ca1[:, :])
    p.barrier()
    p.op("vector", lambda v: v.reduce_sum(pcol[:, 2:3], pri[:, :], axis=AX.X))
    p.barrier()
    p.op("vector", lambda v: v.tensor_scalar(pcol[:, 2:3], pcol[:, 2:3], CAP2, None, ALU.min))
    p.barrier()
    p.op("vector", lambda v: v.tensor_tensor_scan(
        CT[:, :], aux.ones_col[0:8, 0:1].broadcast_to([8, TPC]), M1T[:, :], 0.0,
        ALU.mult, ALU.add))
    p.barrier()
    p.op("vector", lambda v: v.tensor_scalar(CT[:, :], CT[:, :], pcol[:, 0:1], None, ALU.add))
    p.barrier()
    p.op("vector", lambda v: v.tensor_sub(CT[:, :], CT[:, :], M1T[:, :]))
    p.barrier()
    p.op("vector", lambda v: v.tensor_scalar(CT[:, :], CT[:, :], CAP2 - 0.5, None, ALU.is_ge))
    p.barrier()
    p.op("vector", lambda v: v.tensor_scalar(CT[:, :], CT[:, :], -1.0, 1.0, ALU.mult, ALU.add))
    p.barrier()
    p.op("vector", lambda v: v.tensor_mul(G1T[:, :], G1T[:, :], CT[:, :]))
    p.barrier()
    p.op("vector", lambda v: v.tensor_tensor_scan(
        CT[:, :], aux.ones_col[0:8, 0:1].broadcast_to([8, TPC]), M2T[:, :], 0.0,
        ALU.mult, ALU.add))
    p.barrier()
    p.op("vector", lambda v: v.tensor_scalar(CT[:, :], CT[:, :], pcol[:, 1:2], pcol[:, 2:3],
                                             ALU.add, ALU.add))
    p.barrier()
    p.op("vector", lambda v: v.tensor_sub(CT[:, :], CT[:, :], M2T[:, :]))
    p.barrier()
    p.op("vector", lambda v: v.tensor_scalar(CT[:, :], CT[:, :], CAP2 - 0.5, None, ALU.is_ge))
    p.barrier()
    p.op("vector", lambda v: v.tensor_scalar(CT[:, :], CT[:, :], -1.0, 1.0, ALU.mult, ALU.add))
    p.barrier()
    p.op("vector", lambda v: v.tensor_mul(G2T[:, :], G2T[:, :], CT[:, :]))
    p.barrier()
    p.op("vector", lambda v: v.tensor_add(G1T[:, :], G1T[:, :], G2T[:, :]))
    p.barrier()
    for c0 in (0, 512, 1024, 1536):
        p.op("tensor", lambda t, c0=c0: t.matmul(
            aux.ps_f32[0:1, c0:c0 + 512], aux.ones_col[0:8, 0:1], G1T[:, c0:c0 + 512],
            start=True, stop=True))
    p.barrier()
    p.op("vector", lambda v: v.tensor_scalar(wrow[0:1, :], aux.ps_f32[0:1, 0:TPC],
                                             1.1920929e-07, None, ALU.max))
    p.barrier()
    p.op("vector", lambda v: v.reciprocal(wrow[0:1, :], wrow[0:1, :]))
    p.barrier()
    for c0 in (0, 512, 1024, 1536):
        p.op("tensor", lambda t, c0=c0: t.matmul(
            aux.ps_f32[0:8, c0:c0 + 512], aux.ones_row[0:1, 0:8], wrow[0:1, c0:c0 + 512],
            start=True, stop=True))
    p.barrier()
    for c0 in (0, 512, 1024, 1536):
        p.op("vector", lambda v, c0=c0: v.tensor_mul(
            G1T[:, c0:c0 + 512], G1T[:, c0:c0 + 512], aux.ps_f32[0:8, c0:c0 + 512]))
    p.barrier()
    p.dma("gpsimd", o_WT[:, :], G1T[:, :])
    p.barrier()
    p.finish()
    return p


def build_l2d2():
    p = Prog("l2d2")
    i_F = p.input("Fown", (8, 128, NDT * TPC), BF16)
    i_WT = p.input("WTs", (8, TPC))
    i_xn = p.input("xn2", (128, NDT * TPC), BF16)
    i_xa = p.input("xa", (128, NDT * TPC))
    sw1 = p.input("sw1", (D, 4 * D))
    sb1c = p.input("sb1c", (128, 32))
    sw2 = p.input("sw2", (4 * D, D))
    sb2c = p.input("sb2c", (128, NDT))
    o_x3 = p.output("x3", (128, NDT * TPC))

    aux = Aux(p)
    YB = p.sb("d2_yb", [128, NDT * TPC], F32)
    Fe = p.sb("d2_fe", [128, NDT * TPC], BF16)
    XN = p.sb("d2_xn", [128, NDT * TPC], BF16)
    HS = p.sb("d2_hs", [128, 32 * 512], BF16)
    wkst = [p.sb(f"d2_wk{i}", [128, NDT * 128], BF16) for i in range(2)]
    wvst = [p.sb(f"d2_wv{i}", [128, 32 * 128], BF16) for i in range(2)]
    bc1 = p.sb("d2_b1c", [128, 32], F32)
    bc2 = p.sb("d2_b2c", [128, NDT], F32)
    wrow = p.sb("d2_wrow", [1, 512], F32)
    TMP = p.sb("d2_tmp", [128, 512], F32)
    OB = p.sb("d2_ob", [128, 512], F32)
    XAB = p.sb("d2_xab", [128, 512], F32)

    p.start()
    aux.init_phase()
    p.dma("sync", XN[:, :], i_xn[:, :])
    p.dma("scalar", bc1[:, :], sb1c[:, :])
    p.dma("scalar", bc2[:, :], sb2c[:, :])
    p.dma("sync", Fe[:, :], i_F[0, :, :])
    p.barrier()
    for e in range(8):
        for ch in range(4):
            c0 = ch * 512
            p.dma("scalar", wrow[0:1, :], i_WT[e:e + 1, c0:c0 + 512])
            p.barrier()
            p.op("tensor", lambda t: t.matmul(
                aux.ps_f32[:, 1536:2048], aux.ones_row[0:1, :], wrow[0:1, :],
                start=True, stop=True))
            p.barrier()
            for dt in range(NDT):
                p.op("vector", lambda v, dt=dt, c0=c0: v.tensor_mul(
                    TMP[:, :], Fe[:, dt * TPC + c0: dt * TPC + c0 + 512],
                    aux.ps_f32[:, 1536:2048]))
                p.barrier()
                if e == 0:
                    p.op("vector", lambda v, dt=dt, c0=c0: v.tensor_copy(
                        YB[:, dt * TPC + c0: dt * TPC + c0 + 512], TMP[:, :]))
                else:
                    p.op("vector", lambda v, dt=dt, c0=c0: v.tensor_add(
                        YB[:, dt * TPC + c0: dt * TPC + c0 + 512],
                        YB[:, dt * TPC + c0: dt * TPC + c0 + 512], TMP[:, :]))
                p.barrier()
        if e + 1 < 8:
            p.dma("sync", Fe[:, :], i_F[e + 1, :, :])
            p.barrier()
    for ch in range(4):
        c0 = ch * 512
        if ch == 0:
            p.dma("gpsimd", wkst[0][:, :].rearrange("p (a c) -> p a c", c=128),
                  sw1[:, 0:128].rearrange("(a p) c -> p a c", p=128))
            p.barrier()
        for ft in range(32):
            if ft + 1 < 32 or ch < 3:
                nft = (ft + 1) % 32
                p.dma("gpsimd", wkst[(ft + 1) % 2][:, :].rearrange("p (a c) -> p a c", c=128),
                      sw1[:, nft * 128:(nft + 1) * 128].rearrange("(a p) c -> p a c", p=128))
            pr = (ft % 2) * 512
            for k in range(NDT):
                p.op("tensor", lambda t, ft=ft, c0=c0, k=k, pr=pr: t.matmul(
                    aux.ps_f32[:, pr:pr + 512],
                    wkst[ft % 2][:, k * 128:(k + 1) * 128],
                    XN[:, k * TPC + c0: k * TPC + c0 + 512],
                    start=(k == 0), stop=(k == NDT - 1)))
            p.barrier()
            p.op("scalar", lambda s, ft=ft, pr=pr: s.activation(
                out=HS[:, ft * 512:(ft + 1) * 512], in_=aux.ps_f32[:, pr:pr + 512],
                func=AF.Silu, bias=bc1[:, ft:ft + 1]))
            p.barrier()
        p.dma("gpsimd", wvst[0][:, :].rearrange("p (a c) -> p a c", c=128),
              sw2[:, 0:128].rearrange("(a p) c -> p a c", p=128))
        p.barrier()
        for dto in range(NDT):
            if dto + 1 < NDT:
                p.dma("gpsimd", wvst[(dto + 1) % 2][:, :].rearrange("p (a c) -> p a c", c=128),
                      sw2[:, (dto + 1) * 128:(dto + 2) * 128].rearrange("(a p) c -> p a c", p=128))
            p.dma("sync", XAB[:, :], i_xa[:, dto * TPC + c0: dto * TPC + c0 + 512])
            for ft in range(32):
                p.op("tensor", lambda t, dto=dto, ft=ft: t.matmul(
                    aux.ps_f32[:, 1024:1536],
                    wvst[dto % 2][:, ft * 128:(ft + 1) * 128],
                    HS[:, ft * 512:(ft + 1) * 512],
                    start=(ft == 0), stop=(ft == 31)))
            p.barrier()
            p.op("vector", lambda v, dto=dto: v.tensor_scalar(
                OB[:, :], aux.ps_f32[:, 1024:1536], bc2[:, dto:dto + 1], 0.1,
                ALU.add, ALU.mult))
            p.barrier()
            p.op("vector", lambda v, dto=dto, c0=c0: v.tensor_add(
                OB[:, :], OB[:, :], YB[:, dto * TPC + c0: dto * TPC + c0 + 512]))
            p.barrier()
            p.op("vector", lambda v: v.tensor_add(OB[:, :], OB[:, :], XAB[:, :]))
            p.barrier()
            p.dma("gpsimd", o_x3[:, dto * TPC + c0: dto * TPC + c0 + 512], OB[:, :])
            p.barrier()
    p.finish()
    return p



def _emit_tm_proj(p, aux, XN, wres, brow, OUT, act, tmpa, tmpb, pr_base=0):
    for tt in range(16):
        for ci, c2 in enumerate((0, 512)):
            pr = pr_base + (ci % 2) * 512
            for k in range(NDT):
                p.op("tensor", lambda t, tt=tt, c2=c2, k=k, pr=pr: t.matmul(
                    aux.ps_f32[:, pr:pr + 512],
                    XN[:, k * TPC + tt * 128: k * TPC + (tt + 1) * 128],
                    wres[:, k * 1024 + c2: k * 1024 + c2 + 512],
                    start=(k == 0), stop=False))
            p.op("tensor", lambda t, c2=c2, pr=pr: t.matmul(
                aux.ps_f32[:, pr:pr + 512], aux.ones_row_bf[0:1, :],
                brow[0:1, c2:c2 + 512], start=False, stop=True))
            p.barrier()
            dst = lambda tt=tt, c2=c2: OUT[:, tt * 1024 + c2: tt * 1024 + c2 + 512]
            if act == "silu":
                p.op("scalar", lambda s, tt=tt, c2=c2, pr=pr: s.activation(
                    out=OUT[:, tt * 1024 + c2: tt * 1024 + c2 + 512],
                    in_=aux.ps_f32[:, pr:pr + 512], func=AF.Silu))
                p.barrier()
            else:
                p.op("vector", lambda v, pr=pr: v.tensor_scalar(
                    tmpa[:, :], aux.ps_f32[:, pr:pr + 512], 0.0, None, ALU.max))
                p.op("vector", lambda v, pr=pr: v.tensor_scalar(
                    tmpb[:, :], aux.ps_f32[:, pr:pr + 512], 0.0, None, ALU.min))
                p.barrier()
                p.op("scalar", lambda s: s.activation(out=tmpb[:, :], in_=tmpb[:, :], func=AF.Exp))
                p.barrier()
                p.op("vector", lambda v: v.tensor_add(tmpa[:, :], tmpa[:, :], tmpb[:, :]))
                p.barrier()
                p.op("vector", lambda v, tt=tt, c2=c2: v.tensor_scalar(
                    OUT[:, tt * 1024 + c2: tt * 1024 + c2 + 512],
                    tmpa[:, :], -1.0, None, ALU.add))
                p.barrier()


def _emit_fm_proj_elu(p, aux, XN, wres, bcol, OUT, tmpa, tmpb):
    for dto in range(NDT):
        for ci, c0 in enumerate((0, 512, 1024, 1536)):
            pr = (ci % 2) * 512
            for k in range(NDT):
                p.op("tensor", lambda t, dto=dto, c0=c0, k=k, pr=pr: t.matmul(
                    aux.ps_f32[:, pr:pr + 512],
                    wres[:, k * 1024 + dto * 128: k * 1024 + (dto + 1) * 128],
                    XN[:, k * TPC + c0: k * TPC + c0 + 512],
                    start=(k == 0), stop=(k == NDT - 1)))
            p.barrier()
            p.op("vector", lambda v, dto=dto, pr=pr: v.tensor_scalar(
                tmpa[:, :], aux.ps_f32[:, pr:pr + 512], bcol[:, dto:dto + 1], 0.0,
                ALU.add, ALU.max))
            p.op("vector", lambda v, dto=dto, pr=pr: v.tensor_scalar(
                tmpb[:, :], aux.ps_f32[:, pr:pr + 512], bcol[:, dto:dto + 1], 0.0,
                ALU.add, ALU.min))
            p.barrier()
            p.op("scalar", lambda s: s.activation(out=tmpb[:, :], in_=tmpb[:, :], func=AF.Exp))
            p.barrier()
            p.op("vector", lambda v: v.tensor_add(tmpa[:, :], tmpa[:, :], tmpb[:, :]))
            p.barrier()
            p.op("vector", lambda v, dto=dto, c0=c0: v.tensor_scalar(
                OUT[:, dto * TPC + c0: dto * TPC + c0 + 512],
                tmpa[:, :], -1.0, None, ALU.add))
            p.barrier()


def build_l3a():
    p = Prog("l3a")
    xn = p.input("xn3", (128, NDT * TPC), BF16)
    wqT = p.input("wqT", (D, D))
    wkT = p.input("wkT", (D, D))
    wvT = p.input("wvT", (D, D))
    bq_c = p.input("bq_c", (128, NDT))
    bk_c = p.input("bk_c", (128, NDT))
    bk_r = p.input("bk_r", (1, D))
    bv_r = p.input("bv_r", (1, D))
    o_ot = p.output("ot_loc", (128, 16 * 1024), BF16)
    o_q = p.output("q_fm", (128, NDT * TPC), BF16)
    o_S = p.output("S_tot", (128, NDT * 128))

    aux = Aux(p)
    XN = p.sb("l3a_xn", [128, NDT * TPC], BF16)
    wres = [p.sb(f"l3a_w{i}", [128, NDT * 1024], BF16) for i in range(2)]
    QF = p.sb("l3a_qf", [128, NDT * TPC], BF16)
    KF = p.sb("l3a_kf", [128, NDT * TPC], BF16)
    KT = p.sb("l3a_kt", [128, 16 * 1024], BF16)
    VT = p.sb("l3a_vt", [128, 16 * 1024], BF16)
    S32 = p.sb("l3a_s32", [128, NDT * 128], F32)
    Sbf = p.sb("l3a_sbf", [128, 128], BF16)
    EmS = p.sb("l3a_ems", [128, 128], BF16)
    Ot = p.sb("l3a_ot", [128, 2 * 128], BF16)
    msk = p.sb("l3a_msk", [128, 128], BF16)
    brow = p.sb("l3a_brow", [1, 2 * D], BF16)
    bcol = p.sb("l3a_bcol", [128, 2 * NDT], F32)
    tmpa = p.sb("l3a_ta", [128, 512], F32)
    tmpb = p.sb("l3a_tb", [128, 512], F32)
    mskc = p.const("c_triu", np.triu(np.ones((128, 128), np.float32)))

    p.start()
    aux.init_phase()
    p.dma("sync", XN[:, :], xn[:, :])
    p.dma("gpsimd", msk[:, :], mskc[:, :])
    p.dma("gpsimd", brow[0:1, 0:D], bk_r[0:1, :])
    p.dma("gpsimd", brow[0:1, D:2 * D], bv_r[0:1, :])
    p.dma("scalar", bcol[:, 0:NDT], bq_c[:, :])
    p.dma("scalar", bcol[:, NDT:2 * NDT], bk_c[:, :])
    p.dma("gpsimd", wres[0][:, :].rearrange("p (a c) -> p a c", c=1024),
          wqT.rearrange("(a p) c -> p a c", p=128))
    p.op("vector", lambda v: v.memset(S32[:, :], 0.0))
    p.barrier()
    p.dma("gpsimd", wres[1][:, :].rearrange("p (a c) -> p a c", c=1024),
          wkT.rearrange("(a p) c -> p a c", p=128))
    _emit_fm_proj_elu(p, aux, XN, wres[0], bcol[:, 0:NDT], QF, tmpa, tmpb)
    p.dma("gpsimd", wres[0][:, :].rearrange("p (a c) -> p a c", c=1024),
          wvT.rearrange("(a p) c -> p a c", p=128))
    _emit_fm_proj_elu(p, aux, XN, wres[1], bcol[:, NDT:2 * NDT], KF, tmpa, tmpb)
    _emit_tm_proj(p, aux, XN, wres[1], brow[0:1, 0:D], KT, "elu", tmpa, tmpb)
    _emit_tm_proj(p, aux, XN, wres[0], brow[0:1, D:2 * D], VT, "silu", tmpa, tmpb)
    p.dma("gpsimd", o_q[:, :], QF[:, :])
    p.barrier()
    for h in range(NDT):
        for tt in range(16):
            p.op("tensor", lambda t, h=h, tt=tt: t.matmul(
                aux.ps_f32[:, 0:128],
                KF[:, h * TPC + tt * 128: h * TPC + (tt + 1) * 128],
                QF[:, h * TPC + tt * 128: h * TPC + (tt + 1) * 128],
                start=True, stop=True))
            p.op("vector", lambda v, h=h: v.tensor_copy(Sbf[:, :], S32[:, h * 128:(h + 1) * 128]))
            p.barrier()
            p.op("vector", lambda v: v.tensor_mul(EmS[:, :], aux.ps_f32[:, 0:128], msk[:, :]))
            p.barrier()
            p.op("tensor", lambda t, h=h, tt=tt: t.matmul(
                aux.ps_f32[:, 512:640],
                QF[:, h * TPC + tt * 128: h * TPC + (tt + 1) * 128],
                Sbf[:, :], start=True, stop=False))
            p.op("tensor", lambda t, h=h, tt=tt: t.matmul(
                aux.ps_f32[:, 512:640], EmS[:, :],
                VT[:, tt * 1024 + h * 128: tt * 1024 + (h + 1) * 128],
                start=False, stop=True))
            p.op("tensor", lambda t, h=h, tt=tt: t.matmul(
                aux.ps_f32[:, 1024:1152],
                KT[:, tt * 1024 + h * 128: tt * 1024 + (h + 1) * 128],
                VT[:, tt * 1024 + h * 128: tt * 1024 + (h + 1) * 128],
                start=True, stop=True))
            p.barrier()
            sl = (tt % 2) * 128
            p.op("vector", lambda v, sl=sl: v.tensor_copy(Ot[:, sl:sl + 128], aux.ps_f32[:, 512:640]))
            p.op("vector", lambda v, h=h: v.tensor_add(
                S32[:, h * 128:(h + 1) * 128], S32[:, h * 128:(h + 1) * 128],
                aux.ps_f32[:, 1024:1152]))
            p.barrier()
            p.dma("gpsimd", o_ot[:, tt * 1024 + h * 128: tt * 1024 + (h + 1) * 128],
                  Ot[:, sl:sl + 128])
        p.barrier()
    p.dma("sync", o_S[:, :], S32[:, :])
    p.barrier()
    p.finish()
    return p


def build_l3b():
    p = Prog("l3b")
    xn = p.input("xn3", (128, NDT * TPC), BF16)
    qf_in = p.input("q_fm", (128, NDT * TPC), BF16)
    ot_in = p.input("ot_loc", (128, 16 * 1024), BF16)
    sp_in = p.input("S_prev", (3, 128, NDT * 128))
    wuT = p.input("wuT", (D, D))
    bu_r = p.input("bu_r", (1, D))
    woT = p.input("woT", (D, D))
    bo_c = p.input("bo_c", (128, NDT))
    scale_c = p.input("scale_c", (128, NDT))
    i_x3 = p.input("x3", (128, NDT * TPC))
    o_x4p = p.output("x4p", (128, NDT * TPC))

    aux = Aux(p)
    XN = p.sb("l3b_xn", [128, NDT * TPC], BF16)
    QF = p.sb("l3b_qf", [128, NDT * TPC], BF16)
    OT = p.sb("l3b_ot", [128, 16 * 1024], BF16)
    UT = p.sb("l3b_ut", [128, 16 * 1024], BF16)
    wres = p.sb("l3b_w", [128, NDT * 1024], BF16)
    Sb = p.sb("l3b_sb", [128, 3 * NDT * 128], BF16)
    sqb = p.sb("l3b_sqb", [128, 1024], F32)
    rcol = p.sb("l3b_rcol", [128, 16], F32)
    cols = p.sb("l3b_cols", [128, 2 * NDT], F32)
    brow = p.sb("l3b_brow", [1, D], BF16)
    tmpa = p.sb("l3b_ta", [128, 512], F32)
    tmpb = p.sb("l3b_tb", [128, 512], F32)
    OB = p.sb("l3b_ob", [128, 512], F32)
    X3B = p.sb("l3b_x3b", [128, 512], F32)

    p.start()
    aux.init_phase()
    p.dma("sync", XN[:, :], xn[:, :])
    p.dma("sync", QF[:, :], qf_in[:, :])
    p.dma("sync", OT[:, :], ot_in[:, :])
    for j in range(3):
        p.dma("gpsimd", Sb[:, j * NDT * 128:(j + 1) * NDT * 128], sp_in[j, :, :])
    p.dma("gpsimd", brow[0:1, :], bu_r[0:1, :])
    p.dma("scalar", cols[:, 0:NDT], bo_c[:, :])
    p.dma("scalar", cols[:, NDT:2 * NDT], scale_c[:, :])
    p.dma("gpsimd", wres[:, :].rearrange("p (a c) -> p a c", c=1024),
          wuT.rearrange("(a p) c -> p a c", p=128))
    p.barrier()
    _emit_tm_proj(p, aux, XN, wres, brow[0:1, :], UT, "silu", tmpa, tmpb)
    for h in range(NDT):
        for tt in range(16):
            pr = 512 * (tt % 2)
            for j in range(3):
                p.op("tensor", lambda t, h=h, tt=tt, j=j, pr=pr: t.matmul(
                    aux.ps_f32[:, pr:pr + 128],
                    QF[:, h * TPC + tt * 128: h * TPC + (tt + 1) * 128],
                    Sb[:, (j * NDT + h) * 128: (j * NDT + h + 1) * 128],
                    start=(j == 0), stop=(j == 2)))
            p.barrier()
            p.op("vector", lambda v, h=h, tt=tt, pr=pr: v.tensor_add(
                OT[:, tt * 1024 + h * 128: tt * 1024 + (h + 1) * 128],
                OT[:, tt * 1024 + h * 128: tt * 1024 + (h + 1) * 128],
                aux.ps_f32[:, pr:pr + 128]))
            p.barrier()
    for tt in range(16):
        p.op("scalar", lambda s, tt=tt: s.activation(
            out=sqb[:, :], in_=OT[:, tt * 1024:(tt + 1) * 1024], func=AF.Square))
        p.barrier()
        p.op("vector", lambda v, tt=tt: v.reduce_sum(rcol[:, tt:tt + 1], sqb[:, :], axis=AX.X))
        p.barrier()
    p.op("scalar", lambda s: s.activation(out=rcol[:, :], in_=rcol[:, :], func=AF.Ln))
    p.barrier()
    p.op("scalar", lambda s: s.activation(out=rcol[:, :], in_=rcol[:, :], func=AF.Exp, scale=0.5))
    p.barrier()
    p.op("vector", lambda v: v.tensor_scalar(rcol[:, :], rcol[:, :], 1.0 / 32.0, 1e-8,
                                             ALU.mult, ALU.add))
    p.barrier()
    p.op("vector", lambda v: v.reciprocal(rcol[:, :], rcol[:, :]))
    p.barrier()
    for tt in range(16):
        p.op("vector", lambda v, tt=tt: v.tensor_scalar(
            OT[:, tt * 1024:(tt + 1) * 1024], OT[:, tt * 1024:(tt + 1) * 1024],
            rcol[:, tt:tt + 1], None, ALU.mult))
    p.barrier()
    for tt in range(16):
        p.op("vector", lambda v, tt=tt: v.tensor_mul(
            OT[:, tt * 1024:(tt + 1) * 1024], OT[:, tt * 1024:(tt + 1) * 1024],
            UT[:, tt * 1024:(tt + 1) * 1024]))
    p.barrier()
    p.dma("gpsimd", wres[:, :].rearrange("p (a c) -> p a c", c=1024),
          woT.rearrange("(a p) c -> p a c", p=128))
    groups = []
    cur = []
    for h in range(NDT):
        for tt in range(16):
            cur.append((OT[:, tt * 1024 + h * 128: tt * 1024 + (h + 1) * 128],
                        QF[:, h * TPC + tt * 128: h * TPC + (tt + 1) * 128]))
            if len(cur) == 8:
                groups.append(cur)
                cur = []
    transpose_pipe(p, aux, groups)
    for kd in range(NDT):
        p.op("vector", lambda v, kd=kd: v.tensor_scalar(
            wres[:, kd * 1024:(kd + 1) * 1024], wres[:, kd * 1024:(kd + 1) * 1024],
            cols[:, NDT + kd: NDT + kd + 1], None, ALU.mult))
    p.barrier()
    for dto in range(NDT):
        for ci, c0 in enumerate((0, 512, 1024, 1536)):
            pr = (ci % 2) * 512
            p.dma("sync", X3B[:, :], i_x3[:, dto * TPC + c0: dto * TPC + c0 + 512])
            for k in range(NDT):
                p.op("tensor", lambda t, dto=dto, c0=c0, k=k, pr=pr: t.matmul(
                    aux.ps_f32[:, pr:pr + 512],
                    wres[:, k * 1024 + dto * 128: k * 1024 + (dto + 1) * 128],
                    QF[:, k * TPC + c0: k * TPC + c0 + 512],
                    start=(k == 0), stop=(k == NDT - 1)))
            p.barrier()
            p.op("vector", lambda v, dto=dto, pr=pr: v.tensor_scalar(
                OB[:, :], aux.ps_f32[:, pr:pr + 512], cols[:, dto:dto + 1], None, ALU.add))
            p.barrier()
            p.op("vector", lambda v: v.tensor_add(OB[:, :], OB[:, :], X3B[:, :]))
            p.barrier()
            p.dma("gpsimd", o_x4p[:, dto * TPC + c0: dto * TPC + c0 + 512], OB[:, :])
            p.barrier()
    p.finish()
    return p


def build_l3c():
    p = Prog("l3c")
    xn = p.input("xn3", (128, NDT * TPC), BF16)
    i_u1 = p.input("U1", (D, 256))
    i_v1 = p.input("V1", (256, 4 * D))
    b1c_in = p.input("b1c", (128, 32))
    i_u2 = p.input("U2", (4 * D, 256))
    i_v2 = p.input("V2", (256, D))
    b2c_in = p.input("b2c", (128, NDT))
    i_x4p = p.input("x4p", (128, NDT * TPC))
    o_x4 = p.output("x4", (128, NDT * TPC))

    aux = Aux(p)
    XN = p.sb("l3c_xn", [128, NDT * TPC], BF16)
    U1r = p.sb("l3c_u1", [128, NDT * 256], BF16)
    V1r = p.sb("l3c_v1", [128, 2 * 4 * D], BF16)
    U2r = p.sb("l3c_u2", [128, 32 * 256], BF16)
    V2r = p.sb("l3c_v2", [128, 2 * D], BF16)
    PB = p.sb("l3c_pb", [128, 2 * TPC], BF16)
    QB = p.sb("l3c_qb", [128, 2 * 512], BF16)
    HB = p.sb("l3c_hb", [128, 512], F32)
    Hg = p.sb("l3c_hg", [128, 512], BF16)
    t1 = p.sb("l3c_t1", [128, 512], F32)
    b1c = p.sb("l3c_b1c", [128, 32], F32)
    b2c = p.sb("l3c_b2c", [128, NDT], F32)
    OB = p.sb("l3c_ob", [128, 512], F32)
    XPB = p.sb("l3c_xpb", [128, 512], F32)

    p.start()
    aux.init_phase()
    p.dma("sync", XN[:, :], xn[:, :])
    p.dma("gpsimd", U1r[:, :].rearrange("p (a c) -> p a c", c=256),
          i_u1.rearrange("(a p) c -> p a c", p=128))
    p.dma("gpsimd", V1r[:, :].rearrange("p (a c) -> p a c", c=4 * D),
          i_v1.rearrange("(a p) c -> p a c", p=128))
    p.dma("gpsimd", U2r[:, :].rearrange("p (a c) -> p a c", c=256),
          i_u2.rearrange("(a p) c -> p a c", p=128))
    p.dma("gpsimd", V2r[:, :].rearrange("p (a c) -> p a c", c=D),
          i_v2.rearrange("(a p) c -> p a c", p=128))
    p.dma("scalar", b1c[:, :], b1c_in[:, :])
    p.dma("scalar", b2c[:, :], b2c_in[:, :])
    p.barrier()
    for pd in range(2):
        for ci, c0 in enumerate((0, 512, 1024, 1536)):
            pr = (ci % 2) * 512
            for k in range(NDT):
                p.op("tensor", lambda t, pd=pd, c0=c0, k=k, pr=pr: t.matmul(
                    aux.ps_f32[:, pr:pr + 512],
                    U1r[:, k * 256 + pd * 128: k * 256 + (pd + 1) * 128],
                    XN[:, k * TPC + c0: k * TPC + c0 + 512],
                    start=(k == 0), stop=(k == NDT - 1)))
            p.barrier()
            p.op("vector", lambda v, pd=pd, c0=c0, pr=pr: v.tensor_copy(
                PB[:, pd * TPC + c0: pd * TPC + c0 + 512], aux.ps_f32[:, pr:pr + 512]))
            p.barrier()
    for ch in range(4):
        c0 = ch * 512
        for ft in range(33):
            if ft < 32:
                pr = 1024 + (ft % 2) * 512
                for kd in range(2):
                    p.op("tensor", lambda t, ft=ft, c0=c0, kd=kd, pr=pr: t.matmul(
                        aux.ps_f32[:, pr:pr + 512],
                        V1r[:, kd * 4 * D + ft * 128: kd * 4 * D + (ft + 1) * 128],
                        PB[:, kd * TPC + c0: kd * TPC + c0 + 512],
                        start=(kd == 0), stop=(kd == 1)))
            if ft > 0:
                fp = ft - 1
                prp = 1024 + (fp % 2) * 512
                p.op("vector", lambda v, fp=fp, prp=prp: v.tensor_scalar(
                    HB[:, :], aux.ps_f32[:, prp:prp + 512], b1c[:, fp:fp + 1], None, ALU.add))
                p.barrier()
                p.op("scalar", lambda s: s.activation(out=t1[:, :], in_=HB[:, :], func=AF.Square))
                p.barrier()
                p.op("vector", lambda v: v.tensor_mul(t1[:, :], t1[:, :], HB[:, :]))
                p.barrier()
                p.op("vector", lambda v: v.scalar_tensor_tensor(
                    t1[:, :], t1[:, :], 0.044715, HB[:, :], ALU.mult, ALU.add))
                p.barrier()
                p.op("scalar", lambda s: s.activation(out=t1[:, :], in_=t1[:, :], func=AF.Tanh,
                                                      scale=0.7978845608028654))
                p.barrier()
                p.op("vector", lambda v: v.tensor_scalar(t1[:, :], t1[:, :], 0.5, 0.5,
                                                         ALU.mult, ALU.add))
                p.barrier()
                p.op("vector", lambda v: v.tensor_mul(Hg[:, :], HB[:, :], t1[:, :]))
                p.barrier()
                for qd in range(2):
                    p.op("tensor", lambda t, fp=fp, qd=qd: t.matmul(
                        aux.ps_f32[:, qd * 512:(qd + 1) * 512],
                        U2r[:, fp * 256 + qd * 128: fp * 256 + (qd + 1) * 128],
                        Hg[:, :], start=(fp == 0), stop=(fp == 31)))
            p.barrier()
        for qd in range(2):
            p.op("vector", lambda v, qd=qd: v.tensor_copy(
                QB[:, qd * 512:(qd + 1) * 512], aux.ps_f32[:, qd * 512:(qd + 1) * 512]))
        p.barrier()
        for dto in range(NDT):
            p.dma("sync", XPB[:, :], i_x4p[:, dto * TPC + c0: dto * TPC + c0 + 512])
            for qd in range(2):
                p.op("tensor", lambda t, dto=dto, qd=qd: t.matmul(
                    aux.ps_f32[:, 1024:1536],
                    V2r[:, qd * D + dto * 128: qd * D + (dto + 1) * 128],
                    QB[:, qd * 512:(qd + 1) * 512],
                    start=(qd == 0), stop=(qd == 1)))
            p.barrier()
            p.op("vector", lambda v, dto=dto: v.tensor_scalar(
                OB[:, :], aux.ps_f32[:, 1024:1536], b2c[:, dto:dto + 1], None, ALU.add))
            p.barrier()
            p.op("vector", lambda v: v.tensor_add(OB[:, :], OB[:, :], XPB[:, :]))
            p.barrier()
            p.dma("gpsimd", o_x4[:, dto * TPC + c0: dto * TPC + c0 + 512], OB[:, :])
            p.barrier()
    p.finish()
    return p



def _col8(v):
    return np.ascontiguousarray(np.asarray(v, np.float32).reshape(-1).reshape(8, 128).T)


def _f32(v):
    return np.ascontiguousarray(np.asarray(v, np.float32))


def kernel(**inputs):
    import ml_dtypes
    BFD = ml_dtypes.bfloat16
    I = {k: np.asarray(v) for k, v in inputs.items()}
    x = _f32(I["L0_attn_x"])
    pln = _get_prog("ln", build_ln)

    fms0 = host_fm(x)
    rln0 = pln.run([{"x_fm": a} for a in fms0])
    p0 = _get_prog("l0", build_l0)
    maps = []
    for c in range(NCORE):
        b, s = c // 4, c % 4
        grp = range(4) if b == 0 else range(4, 8)
        xng = np.concatenate([np.asarray(rln0[i]["xn_tm"]) for i in grp], axis=0)
        maps.append({
            "x_fm": fms0[c], "xn_fm": np.asarray(rln0[c]["xn_fm"]), "xn_g": xng,
            "m_own_parts": np.stack([np.asarray(rln0[s]["m_part"]),
                                     np.asarray(rln0[4 + s]["m_part"])]),
            "m_all_parts": np.stack([np.asarray(rln0[i]["m_part"]) for i in range(8)]),
            "w_win": _f32(I["L0_ffn_Win"]),
            "w_bin": _f32(I["L0_ffn_bin_"]).reshape(64, 1),
            "w_W1": _f32(I["L0_ffn_W1"]), "w_b1": _f32(I["L0_ffn_b1"]),
            "w_W2": _f32(I["L0_ffn_W2"]),
            "w_b2": _f32(I["L0_ffn_b2"]).reshape(64, 1),
            "w_wout": _f32(I["L0_ffn_Wout"]),
            "w_bout": np.ascontiguousarray(_f32(I["L0_ffn_bout"]).reshape(8, 128).T),
        })
    r0 = p0.run(maps)
    x1 = [np.asarray(r0[c]["x1_fm"]) for c in range(NCORE)]

    rln1 = pln.run([{"x_fm": a} for a in x1])
    xn1 = [np.asarray(rln1[c]["xn_fm"]) for c in range(NCORE)]
    Wat = {k: _f32(I["L1_attn_" + k]) for k in
           ("in_proj_w", "conv_w", "conv_b", "dt_bias", "A_log", "D_param",
            "fc_D_w", "norm_w", "out_proj_w")}
    attn1 = host_l1_attn(xn1, Wat)

    def _wl(l):
        return {"wqkv": _f32(I["L1_ffn_wqkv"])[l], "qs": _f32(I["L1_ffn_q_scale"])[l],
                "ks": _f32(I["L1_ffn_k_scale"])[l], "wo": _f32(I["L1_ffn_wo"])[l],
                "lng": _f32(I["L1_ffn_ln_g"])[l], "lnb": _f32(I["L1_ffn_ln_b"])[l],
                "w1": _f32(I["L1_ffn_w1"])[l], "b1": _f32(I["L1_ffn_b1"])[l],
                "w2": _f32(I["L1_ffn_w2"])[l], "b2": _f32(I["L1_ffn_b2"])[l]}
    y1 = host_l1f(xn1, _wl(0), None, None)
    x2 = host_l1f(y1, _wl(1), x1, attn1)

    rln2 = pln.run([{"x_fm": a} for a in x2])
    xn2 = [np.asarray(rln2[c]["xn_fm"]) for c in range(NCORE)]
    lnr2 = [np.asarray(rln2[c]["lnrows"]).reshape(2, TPC) for c in range(NCORE)]
    p2a = _get_prog("l2a", build_l2a)
    zeros4 = np.zeros((128, 4), BFD)
    wm2 = {
        "w_gatt": _col8(I["L2_attn_ln_att_g"]), "w_batt": _col8(I["L2_attn_ln_att_b"]),
        "w_mixk": _col8(I["L2_attn_att_mix_k"]), "w_mixv": _col8(I["L2_attn_att_mix_v"]),
        "w_mixr": _col8(I["L2_attn_att_mix_r"]),
        "w_wk": _f32(I["L2_attn_att_wk"]), "w_wv": _f32(I["L2_attn_att_wv"]),
        "w_wr": _f32(I["L2_attn_att_wr"]),
        "w_tdec": _col8(I["L2_attn_att_time_decay"]),
        "w_tfir": _col8(I["L2_attn_att_time_first"]),
        "w_router": _f32(I["L2_ffn_router_w"]),
    }
    maps = []
    for c in range(NCORE):
        b, s = c // 4, c % 4
        prv = xn2[c - 1] if s > 0 else None
        cols = []
        for di in range(NDT):
            lo = prv[:, di * TPC + TPC - 4: (di + 1) * TPC] if prv is not None else zeros4
            cols.append(np.concatenate([lo, xn2[c][:, di * TPC:(di + 1) * TPC]], axis=1))
        maps.append({
            "xn2h": np.ascontiguousarray(np.concatenate(cols, axis=1)),
            "bnd": np.full((128, 1), 0.0 if s == 0 else 1.0, np.float32),
            "x2own": x2[c], "lnrows": lnr2[c], **wm2,
        })
    r2a = p2a.run(maps)

    p2d0 = _get_prog("l2d0", build_l2d0)
    maps = []
    for c in range(NCORE):
        cp1 = np.zeros((8, 8), np.float32)
        cp2 = np.zeros((8, 8), np.float32)
        ca1 = np.zeros((8, 8), np.float32)
        for j in range(NCORE):
            ca1[:, j] = np.asarray(r2a[j]["cnt"])[0]
            if j < c:
                cp1[:, j] = np.asarray(r2a[j]["cnt"])[0]
                cp2[:, j] = np.asarray(r2a[j]["cnt"])[1]
        maps.append({"gates_tm": np.asarray(r2a[c]["gates_tm"]),
                     "cntp1T": cp1, "cntp2T": cp2, "cnt1aT": ca1})
    r2d0 = p2d0.run(maps)

    p2b = _get_prog("l2b", build_l2b)
    tdc = _col8(I["L2_attn_att_time_decay"])
    tdr = _f32(I["L2_attn_att_time_decay"]).reshape(1, D)
    maps = []
    for c in range(NCORE):
        b, s = c // 4, c % 4
        E = np.zeros((3, 2, 128, NDT), np.float32)
        for j in range(3):
            src = s - 1 - j
            if src >= 0:
                E[j] = np.asarray(r2a[b * 4 + src]["wkvE"])
        maps.append({
            "P_loc": np.asarray(r2a[c]["P_loc"]), "Q_loc": np.asarray(r2a[c]["Q_loc"]),
            "ku": np.asarray(r2a[c]["ku"]), "kd": np.asarray(r2a[c]["kd"]),
            "r_sig": np.asarray(r2a[c]["r_sig"]), "xn2": xn2[c], "Eprev": E,
            "w_tdec": tdc, "w_tdrow": tdr, "w_wo": _f32(I["L2_attn_att_wo"]),
        })
    r2b = p2b.run(maps)
    xmid = [np.asarray(r2b[c]["xmid"]) for c in range(NCORE)]

    p2d1 = _get_prog("l2d1", build_l2d1)
    xall = np.zeros((128, NDT * NTOK_G), BFD)
    for di in range(NDT):
        for j in range(NCORE):
            xall[:, di * NTOK_G + j * TPC: di * NTOK_G + (j + 1) * TPC] = \
                xn2[j][:, di * TPC:(di + 1) * TPC]
    w1m = _f32(I["L2_ffn_w1"])
    b1m = _f32(I["L2_ffn_b1"])
    w2m = _f32(I["L2_ffn_w2"])
    b2m = _f32(I["L2_ffn_b2"])
    maps = [{"xall": xall, "w1e": w1m[c],
             "b1c": np.ascontiguousarray(b1m[c].reshape(32, 128).T),
             "w2e": w2m[c], "b2c": _col8(b2m[c])} for c in range(NCORE)]
    r2d1 = p2d1.run(maps)

    p2c = _get_prog("l2c", build_l2c)
    zeros4f = np.zeros((128, 4), np.float32)
    maps = []
    for c in range(NCORE):
        b, s = c // 4, c % 4
        prv = xmid[c - 1] if s > 0 else None
        cols = []
        for di in range(NDT):
            lo = prv[:, di * TPC + TPC - 4: (di + 1) * TPC] if prv is not None else zeros4f
            cols.append(np.concatenate([lo, xmid[c][:, di * TPC:(di + 1) * TPC]], axis=1))
        maps.append({
            "xmidh": np.ascontiguousarray(np.concatenate(cols, axis=1)),
            "bnd": np.full((128, 1), 0.0 if s == 0 else 1.0, np.float32),
            "w_gffn": _col8(I["L2_attn_ln_ffn_g"]), "w_bffn": _col8(I["L2_attn_ln_ffn_b"]),
            "w_mixk2": _col8(I["L2_attn_ffn_mix_k"]), "w_mixr2": _col8(I["L2_attn_ffn_mix_r"]),
            "w_wkf": _f32(I["L2_attn_ffn_wk"]), "w_wvf": _f32(I["L2_attn_ffn_wv"]),
            "w_wrf": _f32(I["L2_attn_ffn_wr"]), "x2": x2[c],
        })
    r2c = p2c.run(maps)

    p2d2 = _get_prog("l2d2", build_l2d2)
    maps = []
    for c in range(NCORE):
        Fown = np.zeros((8, 128, NDT * TPC), BFD)
        for e in range(8):
            Fe = np.asarray(r2d1[e]["F"])
            for di in range(NDT):
                Fown[e][:, di * TPC:(di + 1) * TPC] = \
                    Fe[:, di * NTOK_G + c * TPC: di * NTOK_G + (c + 1) * TPC]
        maps.append({
            "Fown": Fown, "WTs": np.asarray(r2d0[c]["WTs"]), "xn2": xn2[c],
            "xa": np.asarray(r2c[c]["xa"]),
            "sw1": _f32(I["L2_ffn_sw1"]),
            "sb1c": np.ascontiguousarray(_f32(I["L2_ffn_sb1"]).reshape(32, 128).T),
            "sw2": _f32(I["L2_ffn_sw2"]), "sb2c": _col8(I["L2_ffn_sb2"]),
        })
    r2d2 = p2d2.run(maps)
    x3 = [np.asarray(r2d2[c]["x3"]) for c in range(NCORE)]

    rln3 = pln.run([{"x_fm": a} for a in x3])
    xn3 = [np.asarray(rln3[c]["xn_fm"]) for c in range(NCORE)]
    p3a = _get_prog("l3a", build_l3a)
    wm3 = {
        "wqT": np.ascontiguousarray(_f32(I["L3_attn_Wq"]).T),
        "wkT": np.ascontiguousarray(_f32(I["L3_attn_Wk"]).T),
        "wvT": np.ascontiguousarray(_f32(I["L3_attn_Wv"]).T),
        "bq_c": _col8(I["L3_attn_bq"]), "bk_c": _col8(I["L3_attn_bk"]),
        "bk_r": _f32(I["L3_attn_bk"]).reshape(1, D),
        "bv_r": _f32(I["L3_attn_bv"]).reshape(1, D),
    }
    r3a = p3a.run([{"xn3": xn3[c], **wm3} for c in range(NCORE)])

    p3b = _get_prog("l3b", build_l3b)
    wm3b = {
        "wuT": np.ascontiguousarray(_f32(I["L3_attn_Wu"]).T),
        "bu_r": _f32(I["L3_attn_bu"]).reshape(1, D),
        "woT": np.ascontiguousarray(_f32(I["L3_attn_Wo"]).T),
        "bo_c": _col8(I["L3_attn_bo"]), "scale_c": _col8(I["L3_attn_scale"]),
    }
    maps = []
    for c in range(NCORE):
        b, s = c // 4, c % 4
        Sp = np.zeros((3, 128, NDT * 128), np.float32)
        for j in range(3):
            src = s - 1 - j
            if src >= 0:
                Sp[j] = np.asarray(r3a[b * 4 + src]["S_tot"])
        maps.append({"xn3": xn3[c], "q_fm": np.asarray(r3a[c]["q_fm"]),
                     "ot_loc": np.asarray(r3a[c]["ot_loc"]), "S_prev": Sp,
                     "x3": x3[c], **wm3b})
    r3b = p3b.run(maps)

    p3c = _get_prog("l3c", build_l3c)
    wm3c = {
        "U1": _f32(I["L3_ffn_U1"]), "V1": _f32(I["L3_ffn_V1"]),
        "b1c": np.ascontiguousarray(_f32(I["L3_ffn_b1"]).reshape(32, 128).T),
        "U2": _f32(I["L3_ffn_U2"]), "V2": _f32(I["L3_ffn_V2"]),
        "b2c": _col8(I["L3_ffn_b2"]),
    }
    r3c = p3c.run([{"xn3": xn3[c], "x4p": np.asarray(r3b[c]["x4p"]), **wm3c}
                   for c in range(NCORE)])

    out = np.zeros((B, L, D), np.float32)
    for c in range(NCORE):
        b, sl = _slice_of_core(c)
        out[b, sl] = unfm(np.asarray(r3c[c]["x4"])).T
    return out
